# version 21
# baseline (speedup 1.0000x reference)
import math
import numpy as np
import ml_dtypes
import concourse.bass as bass
import concourse.mybir as mybir
from concourse.bass_utils import run_bass_kernel_spmd

F32 = mybir.dt.float32
F16 = mybir.dt.float16
BF16 = mybir.dt.bfloat16
I32 = mybir.dt.int32
AF = mybir.ActivationFunctionType
ALU = mybir.AluOpType
AX = mybir.AxisListType
bf16 = ml_dtypes.bfloat16

B, T, D, H, DH, INNER = 4, 2048, 2048, 16, 128, 8192
NQ = 8
NKB = [1, 1, 2, 2, 3, 3, 4, 4]
SEGS = [1, 1, 1, 1, 2, 2, 2, 2]
ISQD = 1.0 / math.sqrt(DH)
BIGC = 384.0
LNEPS = 1e-5
QEPS = 1e-5


class Sem:
    def __init__(self, h, name):
        self.h = h
        self.n = 0
        self.name = name

    def inc(self, k=1):
        self.n += k
        return self.n


class Prog:
    ENGS = ("sync", "gpsimd", "vector", "scalar", "tensor")

    def __init__(self):
        self.ops = {e: [] for e in self.ENGS}

    def emit(self, eng, fn, waits=(), inc=None, inc_k=1):
        tok = None
        incs = []
        if inc is not None:
            v = inc.inc(inc_k)
            incs = [(inc, inc_k)]
            tok = (inc, v)
        self.ops[eng].append((tuple(w for w in waits if w is not None),
                              fn, incs))
        return tok

    def check_deadlock(self):
        pcs = {e: 0 for e in self.ENGS}
        cnt = {}
        total = sum(len(v) for v in self.ops.values())
        done = 0
        progress = True
        while progress:
            progress = False
            for e in self.ENGS:
                while pcs[e] < len(self.ops[e]):
                    waits, _, incs = self.ops[e][pcs[e]]
                    if all(cnt.get(id(s), 0) >= v for s, v in waits):
                        for s, k in incs:
                            cnt[id(s)] = cnt.get(id(s), 0) + k
                        pcs[e] += 1
                        done += 1
                        progress = True
                    else:
                        break
        if done != total:
            detail = []
            for e in self.ENGS:
                if pcs[e] < len(self.ops[e]):
                    waits = self.ops[e][pcs[e]][0]
                    detail.append(
                        f"{e}@{pcs[e]}/{len(self.ops[e])}: waiting "
                        + str([(s.name, v, cnt.get(id(s), 0))
                               for s, v in waits]))
            raise RuntimeError("build-time deadlock:\n" + "\n".join(detail))

    def run_bodies(self, block):
        def body(name):
            def f(eng):
                for waits, fn, incs in self.ops[name]:
                    for s, v in waits:
                        eng.wait_ge(s.h, v)
                    inst = fn(eng)
                    for s, k in incs:
                        inst.then_inc(s.h, k)
            return f
        block.sync(body("sync"))
        block.gpsimd(body("gpsimd"))
        block.vector(body("vector"))
        block.scalar(body("scalar"))
        block.tensor(body("tensor"))


class Builder:
    def __init__(self, nc, st, taps, upto=2):
        self.nc = nc
        self.st = st
        self.taps = taps
        self.upto = upto
        self.P = Prog()
        self.d_taps = {}
        self.din = {}
        for nm, shp in [("xf", [T, D]), ("xq", [1024, D]), ("wq1", [D, D]),
                        ("wq2", [D, D]), ("wk", [D, D]), ("wv", [D, D]),
                        ("wo", [D, D]), ("wlam", [D, H]),
                        ("w1t", [D, INNER]), ("w2t", [INNER, D])]:
            self.din[nm] = nc.dram_tensor(nm, shp, F32,
                                          kind="ExternalInput").ap()
        self.d_id = nc.dram_tensor("ident", [128, 128], BF16,
                                   kind="ExternalInput").ap()
        self.d_cm = nc.dram_tensor("cmask", [NQ * 128, 512], BF16,
                                   kind="ExternalInput").ap()
        self.d_out = nc.dram_tensor("out", [1024, D], F32,
                                    kind="ExternalOutput").ap()
        self.d_sxqt = nc.dram_tensor("sp_xqt", [D, 1024], BF16,
                                     kind="Internal").ap()
        self.d_spart = nc.dram_tensor("sp_part", [1024, D], F32,
                                      kind="Internal").ap()

    def sb(self, n, s, d, stack=None):
        return (stack or self.st).enter_context(self.nc.sbuf_tensor(n, s, d))

    def psb(self, n, s, d, stack=None):
        return (stack or self.st).enter_context(self.nc.psum_tensor(n, s, d))

    def mktap(self, name, shape, dtype):
        if name in self.taps:
            ap = self.nc.dram_tensor("tap_" + name, shape, dtype,
                                     kind="ExternalOutput").ap()
            self.d_taps[name] = ap
            return ap
        return None

    def setup_sems(self):
        mk = lambda n: Sem(self.st.enter_context(self.nc.semaphore(n)), n)
        self.all_dma_sems = []
        self.dSY, self.dGP, self.dSC = mk("dSY"), mk("dGP"), mk("dSC")
        self.all_dma_sems += [self.dSY, self.dGP, self.dSC]
        self.sV, self.sA, self.sT, self.sP = (mk("sV"), mk("sA"), mk("sT"),
                                              mk("sP"))

    def dma_fence_waits(self):
        return tuple((s, s.n) for s in self.all_dma_sems)

    def ev(self, fn, waits=(), inc=True):
        return self.P.emit("vector", fn, waits, self.sV if inc else None)

    def ea(self, fn, waits=(), inc=True):
        return self.P.emit("scalar", fn, waits, self.sA if inc else None)

    def et(self, fn, waits=(), inc=True):
        return self.P.emit("tensor", fn, waits, self.sT if inc else None)

    def ep(self, fn, waits=(), inc=True):
        return self.P.emit("gpsimd", fn, waits, self.sP if inc else None)

    def edma(self, q, out, in_, waits=(), sem=None):
        if sem is None:
            sem = {"sync": self.dSY, "gpsimd": self.dGP,
                   "scalar": self.dSC}[q]
        return self.P.emit(q, (lambda o, i: lambda eng: eng.dma_start(
            out=o, in_=i))(out, in_), waits, inc=sem, inc_k=16)

    def mkdsem(self, n):
        s = Sem(self.st.enter_context(self.nc.semaphore(n)), n)
        self.all_dma_sems.append(s)
        return s

    def vgap(self, waits=()):
        j = self.junk1
        return self.ev(lambda e: e.tensor_scalar_mul(
            j[:, 0:1], j[:, 1:2], 1.0), waits)

    def agap(self, waits=()):
        j = self.junk1
        return self.ea(lambda e: e.copy(j[:, 2:3], j[:, 3:4]), waits)

    def pgap(self, waits=()):
        j = self.junk1
        return self.ep(lambda e: e.tensor_scalar(
            j[:, 4:5], j[:, 5:6], 1.0, None, ALU.mult), waits)

    def slot(self, n=1):
        a = self._sl
        self._sl += n
        assert self._sl <= self.NSM, "small pool overflow"
        return self.smp[:, a:a + n]

    def build(self):
        self.setup_sems()
        nc = self.nc
        self.junk1 = self.sb("junk1", [128, 8], F32)
        self.ident = self.sb("identsb", [128, 128], BF16)
        self.onesc = self.sb("onesc", [128, 1], F32)
        self.onesr = self.sb("onesr", [1, 128], F32)
        self.NSM = 264
        self.smp = self.sb("smp", [128, self.NSM], F32)
        self._sl = 0
        self.epsb = self.slot()
        self.fq_sxo = self.slot(8)
        self.g1v = self.slot(8)
        self.hsv = self.slot(8)
        self.mw1p = self.slot(16)
        self.mw2p = self.slot(16)
        self.sw1 = self.slot()
        self.sw2 = self.slot()
        self.mean1 = self.slot()
        self.mean2 = self.slot()

        self.s01 = _Scope()
        s01 = self.s01.stack(self.nc)
        self.xnT = self.sb("xnT", [128, 16 * T], BF16, s01)
        self.xnTq = self.sb("xnTq", [128, 16 * 1024], BF16, s01)
        self.yT = self.sb("yT", [128, 16 * 1024], BF16, s01)
        self.lam = self.sb("lam", [128, NQ * H], F32, s01)
        self.masks = self.sb("maskssb", [128, NQ * 512], BF16, s01)

        self.tok_id = self.edma("sync", self.ident[:], self.d_id)
        self.tok_msk = self.edma(
            "sync", self.masks[:].rearrange("p (t n) -> p t n", t=NQ),
            self.d_cm.rearrange("(t p) n -> p t n", p=128))
        self.ev(lambda e: e.memset(self.onesc[:], 1.0), inc=False)
        self.ev(lambda e: e.memset(self.onesr[:], 1.0), inc=False)
        self.ev(lambda e: e.memset(self.junk1[:], 1.0), inc=False)
        self.ev(lambda e: e.memset(self.epsb, LNEPS), inc=False)
        self.vgap()
        self.CONST = self.vgap()

        self.phase0()
        if self.upto >= 1:
            self.phase1()
        else:
            self.P1F = self.P0F
        self.emit_taps()
        self.s01.close()
        if self.upto >= 2:
            self.phase2()
        else:
            zb = self.sb("zstub", [128, D], F32)
            tz = self.ev(lambda e: e.memset(zb[:], 0.0), waits=self.P1F)
            for tt in range(8):
                self.edma("sync", self.d_out[tt * 128:(tt + 1) * 128, :],
                          zb[:], waits=(tz,))
            self.P.emit("sync", lambda e: e.nop(),
                        waits=self.dma_fence_waits(), inc=self.sP)
        self.P.check_deadlock()
        blk = self.st.enter_context(nc.Block())
        self.P.run_bodies(blk)

    def xnT_c(self, c, sl=slice(None)):
        return self.xnT[:, c * T:(c + 1) * T][:, sl]

    def xnTq_c(self, c, sl=slice(None)):
        return self.xnTq[:, c * 1024:(c + 1) * 1024][:, sl]

    def yT_c(self, c, sl=slice(None)):
        return self.yT[:, c * 1024:(c + 1) * 1024][:, sl]

    def phase0(self):
        p0 = self.st.enter_context(_Scope())
        sb0 = lambda n, s, d: self.sb(n, s, d, p0.stack(self.nc))
        xbuf = [sb0(f"xb{i}", [128, D], F32) for i in range(2)]
        sqscr = sb0("sqscr", [128, D], BF16)
        xnb = sb0("xnb", [128, D], BF16)
        xnq32 = sb0("xnq32", [128, D], F32)
        q_u = sb0("q_u", [128, 1024], F32)
        q_ui = sb0("q_ui", [128, 1024], I32)
        q_c1 = sb0("q_c1", [128, 1024], F32)
        xqb = sb0("xqb", [128, D], BF16)
        stg = sb0("stg", [128, D], BF16)
        st24 = self.slot(24)
        ptrs = [self.psb("ptr0", [128, 512], BF16, p0.stack(self.nc)),
                self.psb("ptr1", [128, 512], BF16, p0.stack(self.nc))]

        def stt_(q, k):
            return st24[:, q * 12 + k:q * 12 + k + 1]

        tiles = [("f", i) for i in range(16)] + [("q", j) for j in range(NQ)]
        dsx = [self.mkdsem("dsx0"), self.mkdsem("dsx1")]
        dstg = self.mkdsem("dstg")
        tok_consumed = {}
        ptr_war = [None, None]
        prev_xnb_free = None
        prev_q_free = None

        for idx, (kind, i) in enumerate(tiles):
            q = idx % 2
            src = (self.din["xf"] if kind == "f" else self.din["xq"])[
                i * 128:(i + 1) * 128, :]
            dtok = self.edma("sync", xbuf[q][:], src,
                             waits=(tok_consumed.get(idx - 2),),
                             sem=dsx[q])
            xb = xbuf[q]
            tA = self.ea(lambda e, xb=xb, q=q: e.activation(
                sqscr[:], xb[:], AF.Square, accum_out=stt_(q, 1)),
                waits=(dtok, prev_xnb_free))
            tV = self.ev(lambda e, xb=xb, q=q: e.reduce_sum(
                stt_(q, 0), xb[:], AX.X), waits=(dtok, tA))
            self.vgap()
            self.ev(lambda e, q=q: e.tensor_scalar_mul(
                stt_(q, 2), stt_(q, 0), -1.0 / D), inc=False)
            self.vgap()
            self.ev(lambda e, q=q: e.tensor_tensor(
                stt_(q, 11), stt_(q, 2), stt_(q, 2), ALU.mult), inc=False)
            self.ev(lambda e, q=q: e.tensor_scalar(
                stt_(q, 3), stt_(q, 1), 1.0 / D, None, ALU.mult), inc=False)
            self.vgap()
            self.ev(lambda e, q=q: e.tensor_sub(
                stt_(q, 3), stt_(q, 3), stt_(q, 11)), inc=False)
            self.vgap()
            tV2 = self.ev(lambda e, q=q: e.tensor_scalar_max(
                stt_(q, 3), stt_(q, 3), 0.0))
            self.vgap()
            self.vgap()
            self.ea(lambda e, q=q: e.activation(
                stt_(q, 11), stt_(q, 3), AF.Sqrt, bias=self.epsb),
                waits=(tV2,), inc=False)
            tA3 = self.ea(lambda e, q=q: e.activation(
                stt_(q, 6), stt_(q, 3), AF.Sqrt))
            self.agap()
            self.ev(lambda e, q=q: e.reciprocal(stt_(q, 4), stt_(q, 11)),
                    waits=(tA3,), inc=False)
            self.vgap()
            self.ev(lambda e, q=q: e.tensor_tensor(
                stt_(q, 5), stt_(q, 2), stt_(q, 4), ALU.mult), inc=False)
            self.vgap()
            tV3 = self.vgap()
            tapb = self.ea(lambda e, xb=xb, q=q: e.activation(
                xnb[:], xb[:], AF.Identity, bias=stt_(q, 5),
                scale=stt_(q, 4)), waits=(tV3, prev_xnb_free))
            tok_consumed[idx] = tapb
            tapf = None
            if kind == "q":
                tapf = self.ea(lambda e, xb=xb, q=q: e.activation(
                    xnq32[:], xb[:], AF.Identity, bias=stt_(q, 5),
                    scale=stt_(q, 4)), waits=(prev_q_free,))
                tok_consumed[idx] = tapf

            W = T if kind == "f" else 1024
            dstT = self.xnT if kind == "f" else self.xnTq
            last_tt = None
            for g in range(4):
                pp = ptrs[g % 2]
                tts = None
                for c4 in range(4):
                    c = g * 4 + c4
                    w = ([tapb, self.CONST, self.tok_id, ptr_war[g % 2]]
                         if c4 == 0 else [])
                    tts = self.et(lambda e, c=c, c4=c4, pp=pp: e.transpose(
                        pp[:, c4 * 128:(c4 + 1) * 128],
                        xnb[:, c * 128:(c + 1) * 128], self.ident[:]),
                        waits=w, inc=(c4 == 3))
                dst = dstT[:].rearrange("p (c t) -> p c t", c=16)[
                    :, g * 4:(g + 1) * 4, i * 128:(i + 1) * 128]
                tev = self.ev(lambda e, dst=dst, pp=pp: e.tensor_copy(
                    dst, pp[:, 0:512].rearrange("p (c n) -> p c n", c=4)),
                    waits=(tts,))
                ptr_war[g % 2] = tev
                last_tt = tts
            prev_xnb_free = last_tt

            if kind == "q":
                j = i
                tS = self.ev(lambda e, q=q: e.reduce_max(
                    stt_(q, 7), xnq32[:], AX.X, apply_absolute_value=True),
                    waits=(tapf,))
                self.vgap()
                self.ev(lambda e, q=q: e.tensor_tensor(
                    stt_(q, 8), stt_(q, 6), stt_(q, 4), ALU.mult), inc=False)
                self.vgap()
                self.ev(lambda e, q=q: e.tensor_scalar(
                    stt_(q, 8), stt_(q, 8), math.sqrt(D), 1e-12, ALU.mult,
                    ALU.max), inc=False)
                self.vgap()
                self.ev(lambda e, q=q: e.reciprocal(stt_(q, 10), stt_(q, 8)),
                        inc=False)
                self.vgap()
                self.ev(lambda e, q=q: e.tensor_tensor(
                    stt_(q, 9), stt_(q, 7), stt_(q, 10), ALU.mult),
                    waits=(tS,), inc=False)
                self.vgap()
                self.ev(lambda e, q=q: e.tensor_scalar(
                    stt_(q, 9), stt_(q, 9), math.sqrt(D), QEPS, ALU.mult,
                    ALU.max), inc=False)
                self.vgap()
                self.ev(lambda e, j=j, q=q: e.tensor_copy(
                    self.fq_sxo[:, j:j + 1], stt_(q, 9)), inc=False)
                self.vgap()
                self.ev(lambda e, q=q: e.reciprocal(stt_(q, 11), stt_(q, 9)),
                        inc=False)
                self.vgap()
                self.ev(lambda e, q=q: e.tensor_tensor(
                    stt_(q, 11), stt_(q, 11), stt_(q, 10), ALU.mult),
                    inc=False)
                self.vgap()
                tSc = self.ev(lambda e, q=q: e.tensor_scalar_mul(
                    stt_(q, 11), stt_(q, 11), 127.0 * math.sqrt(D)))
                self.vgap()
                self.vgap()
                tQ = None
                for hh in range(2):
                    hsl = slice(hh * 1024, (hh + 1) * 1024)
                    self.ev(lambda e, q=q, hsl=hsl: e.tensor_scalar(
                        q_u[:], xnq32[:, hsl], stt_(q, 11), BIGC + 0.5,
                        ALU.mult, ALU.add),
                        waits=(tSc, prev_q_free) if hh == 0 else (),
                        inc=False)
                    self.ev(lambda e: e.tensor_copy(q_ui[:], q_u[:]),
                            inc=False)
                    self.ev(lambda e: e.tensor_copy(q_c1[:], q_ui[:]),
                            inc=False)
                    self.ev(lambda e: e.tensor_tensor(
                        q_u[:], q_c1[:], q_u[:], ALU.is_gt), inc=False)
                    tQ = self.ev(lambda e, hsl=hsl: e.scalar_tensor_tensor(
                        xqb[:, hsl], q_c1[:], -BIGC, q_u[:], ALU.add,
                        ALU.subtract))
                tstg = None
                for g in range(4):
                    pp = ptrs[g % 2]
                    tts = None
                    for c4 in range(4):
                        c = g * 4 + c4
                        w = ([tQ, prev_xnb_free, ptr_war[g % 2]]
                             if c4 == 0 else [])
                        tts = self.et(
                            lambda e, c=c, c4=c4, pp=pp: e.transpose(
                                pp[:, c4 * 128:(c4 + 1) * 128],
                                xqb[:, c * 128:(c + 1) * 128],
                                self.ident[:]),
                            waits=w, inc=(c4 == 3))
                    tev = self.ev(lambda e, g=g, pp=pp: e.tensor_copy(
                        stg[:, g * 512:(g + 1) * 512], pp[:, 0:512]),
                        waits=(tts,))
                    ptr_war[g % 2] = tev
                    tstg = tev
                prev_q_free = self.edma(
                    "gpsimd",
                    self.d_sxqt.rearrange("(c p) n -> p c n", p=128)[
                        :, :, j * 128:(j + 1) * 128],
                    stg[:].rearrange("p (c n) -> p c n", c=16),
                    waits=(tstg,), sem=dstg)

        self.vgap()
        f_v = self.vgap()
        self.agap()
        f_a = self.agap()
        f_t = self.et(lambda e: e.transpose(
            ptrs[0][:, 0:128], self.ident[:], self.ident[:]),
            waits=(ptr_war[0],))
        f_p = self.pgap()
        f_sy = self.P.emit("sync", lambda e: e.nop(),
                           waits=self.dma_fence_waits(), inc=self.sP)
        self.P0F = (f_v, f_a, f_t, f_p, f_sy)
        p0.close()

    def phase1(self):
        P0F = self.P0F
        nc = self.nc
        p1o = _Scope()
        s1o = p1o.stack(nc)
        p1a = _Scope()
        s1a = p1a.stack(nc)

        mslab = [self.sb(f"msl{i}", [128, 1024], F32, s1o) for i in range(2)]
        mjunk = self.sb("mjunk", [128, 1024], BF16, s1o)
        smp1t = self.sb("smp1", [128, 256], F32, s1o)
        _s1 = [0]

        def slot1(n=1):
            a = _s1[0]
            _s1[0] += n
            assert _s1[0] <= 256
            return smp1t[:, a:a + n]

        nmP = [slot1(16), slot1(16)]
        lP = [slot1(16), slot1(16)]
        e_ = slot1(16)
        d_ = slot1(16)
        lw = slot1(16)
        L_ = slot1(4)
        r_ = slot1(4)
        sf = slot1(16)
        nl = slot1(2)
        nmc = slot1(4)
        nmt = slot1(8)
        msc = slot1(2)

        wrot = [self.sb(f"wrot{i}", [128, 16 * 256], BF16, s1a)
                for i in range(2)]
        q1b = self.sb("q1b", [128, 2 * 1024], BF16, s1a)
        q2b = self.sb("q2b", [128, 2 * 1024], BF16, s1a)
        kbf = [self.sb(f"kbf{i}", [128, T], BF16, s1a) for i in range(2)]
        vbf = [self.sb(f"vbf{i}", [128, T], BF16, s1a) for i in range(2)]
        pbuf = [self.sb(f"pbuf{i}", [128, 512], BF16, s1a) for i in range(3)]
        pTb = [self.sb(f"pTb{i}", [128, 512], BF16, s1a) for i in range(3)]
        ysg = self.sb("ysg", [128, 16 * 128], F32, s1a)
        ystg = self.sb("ystg", [128, 256], BF16, s1a)
        ytmp = [self.sb(f"ytmp{i}", [128, 128], F32, s1a) for i in range(2)]
        wlamb = self.sb("wlamb", [128, 16 * 16], BF16, s1a)

        stash = [self.psb(f"stash{i}", [128, 512], F32, s1a)
                 for i in range(3)]
        trp = [self.psb(f"trp{i}", [128, 512], BF16, s1a) for i in range(2)]
        yp = [self.psb(f"yp{i}", [128, 512], F32, s1a) for i in range(2)]
        prp = self.psb("prp", [128, 512], F32, s1a)

        ev, ea, et, ep, edma = self.ev, self.ea, self.et, self.ep, self.edma
        vgap, agap, pgap = self.vgap, self.agap, self.pgap
        xnT_c, xnTq_c = self.xnT_c, self.xnTq_c
        masks, lam, ident = self.masks, self.lam, self.ident

        mean_srcs = []
        for mat, dsrc in (("1", self.din["w1t"]), ("2", self.din["w2t"])):
            for rc in range(dsrc.tensor.shape[0] // 128):
                for cs in range(dsrc.tensor.shape[1] // 1024):
                    mean_srcs.append(
                        (mat, dsrc[rc * 128:(rc + 1) * 128,
                                   cs * 1024:(cs + 1) * 1024]))
        macc1, macc2 = self.mw1p, self.mw2p
        dmsl = [self.mkdsem("dmsl0"), self.mkdsem("dmsl1")]
        dwr = [self.mkdsem("dwr0"), self.mkdsem("dwr1")]
        dwlam = self.mkdsem("dwlam")
        dwob = [self.mkdsem("dwob0"), self.mkdsem("dwob1")]
        dxpt = [self.mkdsem("dxpt0"), self.mkdsem("dxpt1")]
        dspt = self.mkdsem("dspt")
        mtmp = self.slot(8)
        mst = {"i": 0, "tmp_tok": {}, "done": [], "zeroed": False}

        def emit_mean_step():
            i = mst["i"]
            if i >= len(mean_srcs):
                return
            mst["i"] += 1
            if not mst["zeroed"]:
                ev(lambda e: e.memset(macc1[:, :], 0.0), waits=P0F,
                   inc=False)
                ev(lambda e: e.memset(macc2[:, :], 0.0), inc=False)
                mst["zeroed"] = True
            par = i % 2
            mat, srcap = mean_srcs[i]
            dtok = edma("scalar", mslab[par][:], srcap,
                        waits=(mst["done"][i - 2] if i >= 2 else P0F[1],),
                        sem=dmsl[par])
            ts_ = mtmp[:, i % 8:i % 8 + 1]
            atok = ea(lambda e, par=par, ts_=ts_: e.activation(
                mjunk[:], mslab[par][:], AF.Abs, accum_out=ts_),
                waits=(dtok, mst["tmp_tok"].get(i % 8)))
            mst["done"].append(atok)
            acc = macc1 if mat == "1" else macc2
            k = i % 16
            vtok = ev(lambda e, acc=acc, k=k, ts_=ts_: e.tensor_add(
                acc[:, k:k + 1], acc[:, k:k + 1], ts_), waits=(atok,))
            mst["tmp_tok"][i % 8] = vtok

        tok_wlam = edma("gpsimd", wlamb[:].rearrange(
            "p (c n) -> p c n", c=16),
            self.din["wlam"].rearrange("(c p) n -> p c n", p=128),
            waits=(P0F[4],), sem=dwlam)
        prp_war = None
        for tt in range(8):
            mmt = None
            for ic in range(16):
                w = [P0F[2], tok_wlam, prp_war, self.CONST] if ic == 0 else []
                mmt = et(lambda e, ic=ic, tt=tt: e.matmul(
                    prp[:, 0:16],
                    xnTq_c(ic, slice(tt * 128, (tt + 1) * 128)),
                    wlamb[:, ic * 16:(ic + 1) * 16],
                    start=(ic == 0), stop=(ic == 15)), waits=w,
                    inc=(ic == 15))
            prp_war = ea(lambda e, tt=tt: e.activation(
                lam[:, tt * 16:(tt + 1) * 16], prp[:, 0:16], AF.Sigmoid),
                waits=(mmt,))
        lam_gate = agap()

        w_war = [None, None]
        stash_free = [None, None, None]
        pb_free = [None, None, None]
        pT_tok = [None, None, None]
        yp_free = [None, None]
        trp_war = [None, None]
        nm_war = {}
        l_war = {}
        prev_combine = None
        gseg = [0]

        SEGLIST = []
        for t in range(NQ):
            for sig in range(4):
                for s in range(NKB[t]):
                    SEGLIST.append((t, sig, s))

        def emit_QK(t, sig, s, i, gate):
            par = i % 3
            tp = t % 2
            he = sig >> 1
            qb = q1b if (sig & 1) == 0 else q2b
            kv = sig & 1
            idx = s * 4 + sig
            mmt = et(lambda e, par=par, s=s, he=he, t=t, qb=qb,
                     kv=kv: e.matmul(
                stash[par][:, 0:512],
                qb[:, he * 1024 + t * 128:he * 1024 + (t + 1) * 128],
                kbf[kv][:, s * 512:(s + 1) * 512],
                start=True, stop=True),
                waits=(stash_free[par], gate), inc=True)
            vt = mmt
            if s == NKB[t] - 1:
                vt = ev(lambda e, par=par, t=t: e.tensor_tensor(
                    stash[par][:, 0:512], stash[par][:, 0:512],
                    masks[:, t * 512:(t + 1) * 512], ALU.add),
                    waits=(mmt, self.tok_msk))
            ev(lambda e, par=par, tp=tp, idx=idx: e.reduce_max(
                nmP[tp][:, idx:idx + 1], stash[par][:, 0:512], AX.X,
                negate=True), waits=(vt, nm_war.get((tp, sig, s))),
                inc=False)
            vgap()
            vgap()
            rm2 = vgap()
            ext = ea(lambda e, par=par, tp=tp, idx=idx: e.activation(
                pbuf[par][:], stash[par][:, 0:512], AF.Exp,
                bias=nmP[tp][:, idx:idx + 1], scale=1.0,
                accum_out=lP[tp][:, idx:idx + 1]),
                waits=(rm2, pb_free[par], l_war.get((tp, sig, s))))
            nm_war[(tp, sig, s)] = ext
            stash_free[par] = ext
            return ext

        def emit_TRPV(t, sig, s, i, exp_tok):
            par = i % 3
            tpp = i % 2
            kv = sig & 1
            trt = None
            for kc in range(4):
                w = [exp_tok, trp_war[tpp]] if kc == 0 else []
                trt = et(lambda e, kc=kc, par=par, tpp=tpp: e.transpose(
                    trp[tpp][:, kc * 128:(kc + 1) * 128],
                    pbuf[par][:, kc * 128:(kc + 1) * 128], ident[:]),
                    waits=w, inc=(kc == 3))
            tev = ev(lambda e, par=par, tpp=tpp: e.tensor_copy(
                pTb[par][:], trp[tpp][:]), waits=(trt, pT_tok[par]))
            trp_war[tpp] = tev
            pvt = None
            for kc in range(4):
                w = [tev, yp_free[tpp]] if kc == 0 else []
                pvt = et(lambda e, kc=kc, par=par, s=s, kv=kv, tpp=tpp:
                         e.matmul(
                    yp[tpp][:, 0:128],
                    pTb[par][:, kc * 128:(kc + 1) * 128],
                    vbf[kv][:, (4 * s + kc) * 128:(4 * s + kc + 1) * 128],
                    start=(kc == 0), stop=(kc == 3)),
                    waits=w, inc=(kc == 3))
            pb_free[par] = pvt
            pT_tok[par] = pvt
            idx = s * 4 + sig
            yev = ev(lambda e, tpp=tpp, idx=idx: e.tensor_copy(
                ysg[:, idx * 128:(idx + 1) * 128],
                yp[tpp][:, 0:128]), waits=(pvt,))
            yp_free[tpp] = yev
            return yev

        def emit_combine(pj, t, yev_tok):
            nonlocal prev_combine
            ns = NKB[t]
            tp = t % 2
            nm0 = nmP[tp]
            l0 = lP[tp]
            ag = agap()
            ag = agap()
            if ns == 1:
                ev(lambda e: e.reciprocal(sf[:, 0:4], l0[:, 0:4]),
                   waits=(ag, yev_tok, lam_gate), inc=False)
                vgap()
                vgap()
            else:
                if ns == 2:
                    ev(lambda e: e.tensor_tensor(
                        nmc, nm0[:, 0:4], nm0[:, 4:8], ALU.min),
                        waits=(ag, yev_tok, lam_gate), inc=False)
                elif ns == 3:
                    ev(lambda e: e.tensor_tensor(
                        nmt[:, 0:4], nm0[:, 0:4], nm0[:, 4:8], ALU.min),
                        waits=(ag, yev_tok, lam_gate), inc=False)
                    vgap()
                    vgap()
                    ev(lambda e: e.tensor_tensor(
                        nmc, nmt[:, 0:4], nm0[:, 8:12], ALU.min),
                        inc=False)
                else:
                    ev(lambda e: e.tensor_tensor(
                        nmt[:, 0:4], nm0[:, 0:4], nm0[:, 4:8], ALU.min),
                        waits=(ag, yev_tok, lam_gate), inc=False)
                    ev(lambda e: e.tensor_tensor(
                        nmt[:, 4:8], nm0[:, 8:12], nm0[:, 12:16], ALU.min),
                        waits=(ag,), inc=False)
                    vgap()
                    vgap()
                    ev(lambda e: e.tensor_tensor(
                        nmc, nmt[:, 0:4], nmt[:, 4:8], ALU.min), inc=False)
                vgap()
                vmg = vgap()
                for s in range(ns):
                    ep(lambda e, s=s: e.tensor_sub(
                        e_[:, s * 4:(s + 1) * 4], nmc,
                        nm0[:, s * 4:(s + 1) * 4]),
                        waits=(vmg,) if s == 0 else (), inc=False)
                pgap()
                t1 = pgap()
                ea(lambda e, ns=ns: e.activation(
                    d_[:, 0:4 * ns], e_[:, 0:4 * ns], AF.Exp),
                    waits=(t1,), inc=False)
                ag2 = agap()
                ag2 = agap()
                ep(lambda e, ns=ns: e.tensor_tensor(
                    lw[:, 0:4 * ns], l0[:, 0:4 * ns], d_[:, 0:4 * ns],
                    ALU.mult), waits=(ag2,), inc=False)
                pgap()
                pgap()
                if ns == 2:
                    ep(lambda e: e.tensor_add(L_, lw[:, 0:4], lw[:, 4:8]),
                       inc=False)
                elif ns == 3:
                    ep(lambda e: e.tensor_add(nmt[:, 0:4], lw[:, 0:4],
                                              lw[:, 4:8]), inc=False)
                    pgap()
                    pgap()
                    ep(lambda e: e.tensor_add(L_, nmt[:, 0:4],
                                              lw[:, 8:12]), inc=False)
                else:
                    ep(lambda e: e.tensor_add(nmt[:, 0:4], lw[:, 0:4],
                                              lw[:, 4:8]), inc=False)
                    ep(lambda e: e.tensor_add(nmt[:, 4:8], lw[:, 8:12],
                                              lw[:, 12:16]), inc=False)
                    pgap()
                    pgap()
                    ep(lambda e: e.tensor_add(L_, nmt[:, 0:4],
                                              nmt[:, 4:8]), inc=False)
                pgap()
                pL = pgap()
                ev(lambda e: e.reciprocal(r_, L_), waits=(pL,), inc=False)
                vgap()
                rg = vgap()
                for s in range(ns):
                    ep(lambda e, s=s: e.tensor_tensor(
                        sf[:, s * 4:(s + 1) * 4], d_[:, s * 4:(s + 1) * 4],
                        r_, ALU.mult), waits=(rg,) if s == 0 else (),
                        inc=False)
                pgap()
                pgap()
            if ns == 1:
                for he in range(2):
                    hcol = t * 16 + 2 * pj + he
                    ev(lambda e, he=he, hcol=hcol: e.tensor_scalar_mul(
                        nl[:, he:he + 1], lam[:, hcol:hcol + 1], -1.0),
                        inc=False)
                vgap()
                vgap()
                for s in range(ns):
                    for he in range(2):
                        c = s * 4 + 1 + 2 * he
                        ev(lambda e, c=c, he=he: e.tensor_tensor(
                            sf[:, c:c + 1], sf[:, c:c + 1],
                            nl[:, he:he + 1], ALU.mult), inc=False)
                vgap()
                sfg = vgap()
            else:
                for he in range(2):
                    hcol = t * 16 + 2 * pj + he
                    ep(lambda e, he=he, hcol=hcol: e.tensor_scalar(
                        nl[:, he:he + 1], lam[:, hcol:hcol + 1], -1.0,
                        None, ALU.mult), inc=False)
                pgap()
                pgap()
                for s in range(ns):
                    for he in range(2):
                        c = s * 4 + 1 + 2 * he
                        ep(lambda e, c=c, he=he: e.tensor_tensor(
                            sf[:, c:c + 1], sf[:, c:c + 1],
                            nl[:, he:he + 1], ALU.mult), inc=False)
                pgap()
                pg2 = pgap()
                sfg = vgap(waits=(pg2,))
            toks = []
            for he in range(2):
                tmp = ytmp[he]
                terms = [(s * 4 + 2 * he + j, s)
                         for s in range(ns) for j in range(2)]
                first = True
                tk = None
                for n_, (idx, s) in enumerate(terms):
                    ysl = ysg[:, idx * 128:(idx + 1) * 128]
                    sfc = sf[:, idx:idx + 1]
                    last = (n_ == len(terms) - 1)
                    dst = (ystg[:, he * 128:(he + 1) * 128] if last
                           else tmp[:])
                    if first:
                        tk = ev(lambda e, ysl=ysl, sfc=sfc, dst=dst:
                                e.tensor_scalar_mul(dst, ysl, sfc),
                                waits=(sfg,), inc=last)
                        first = False
                    else:
                        tk = ev(lambda e, ysl=ysl, sfc=sfc, dst=dst,
                                tmp=tmp: e.scalar_tensor_tensor(
                            dst, ysl, sfc, tmp[:], ALU.mult, ALU.add),
                            inc=last)
                toks.append(tk)
            trt = None
            for he in range(2):
                trt = et(lambda e, he=he: e.transpose(
                    trp[0][:, he * 128:(he + 1) * 128],
                    ystg[:, he * 128:(he + 1) * 128], ident[:]),
                    waits=(toks[1], trp_war[0]) if he == 0 else (),
                    inc=(he == 1))
            dst = self.yT[:].rearrange("p (c n) -> p c n", c=16)[
                :, 2 * pj:2 * pj + 2, t * 128:(t + 1) * 128]
            tev = ev(lambda e, dst=dst: e.tensor_copy(
                dst, trp[0][:, 0:256].rearrange("p (c n) -> p c n", c=2)),
                waits=(trt,))
            trp_war[0] = tev
            prev_combine = tev
            for s in range(ns):
                for sig in range(4):
                    l_war[(tp, sig, s)] = tev
            return tev

        for pj in range(8):
            c0 = 2 * pj * 128
            tq1 = edma("gpsimd", wrot[0][:].rearrange(
                "p (c n) -> p c n", c=16),
                self.din["wq1"].rearrange("(c p) n -> p c n", p=128)[
                    :, :, c0:c0 + 256], waits=(w_war[0], P0F[4]),
                sem=dwr[0])
            tq2 = edma("gpsimd", wrot[1][:].rearrange(
                "p (c n) -> p c n", c=16),
                self.din["wq2"].rearrange("(c p) n -> p c n", p=128)[
                    :, :, c0:c0 + 256], waits=(w_war[1], P0F[4]),
                sem=dwr[1])
            lastq1 = None
            for he in range(2):
                for th in range(2):
                    mmt = None
                    for ic in range(16):
                        w = [tq1, prp_war, P0F[2]] if ic == 0 else []
                        mmt = et(lambda e, ic=ic, he=he, th=th: e.matmul(
                            prp[:, 0:512],
                            wrot[0][:, ic * 256 + he * 128:
                                     ic * 256 + (he + 1) * 128],
                            xnTq_c(ic, slice(th * 512, (th + 1) * 512)),
                            start=(ic == 0), stop=(ic == 15)),
                            waits=w, inc=(ic == 15))
                    lastq1 = mmt
                    prp_war = ea(lambda e, he=he, th=th: e.activation(
                        q1b[:, he * 1024 + th * 512:he * 1024 +
                            (th + 1) * 512], prp[:, 0:512], AF.Identity,
                        scale=ISQD), waits=(mmt,))
            w_war[0] = lastq1
            tkA = edma("gpsimd", wrot[0][:].rearrange(
                "p (c n) -> p c n", c=16)[:, :, 0:128],
                self.din["wk"].rearrange("(c p) n -> p c n", p=128)[
                    :, :, pj * 128:(pj + 1) * 128],
                waits=(w_war[0],), sem=dwr[0])
            tkB = edma("gpsimd", wrot[0][:].rearrange(
                "p (c n) -> p c n", c=16)[:, :, 128:256],
                self.din["wk"].rearrange("(c p) n -> p c n", p=128)[
                    :, :, (8 + pj) * 128:(9 + pj) * 128], sem=dwr[0])
            lastq2 = None
            for he in range(2):
                for th in range(2):
                    mmt = None
                    for ic in range(16):
                        w = [tq2, prp_war] if ic == 0 else []
                        mmt = et(lambda e, ic=ic, he=he, th=th: e.matmul(
                            prp[:, 0:512],
                            wrot[1][:, ic * 256 + he * 128:
                                     ic * 256 + (he + 1) * 128],
                            xnTq_c(ic, slice(th * 512, (th + 1) * 512)),
                            start=(ic == 0), stop=(ic == 15)),
                            waits=w, inc=(ic == 15))
                    lastq2 = mmt
                    prp_war = ea(lambda e, he=he, th=th: e.activation(
                        q2b[:, he * 1024 + th * 512:he * 1024 +
                            (th + 1) * 512], prp[:, 0:512], AF.Identity,
                        scale=ISQD), waits=(mmt,))
            w_war[1] = lastq2
            tvA = edma("gpsimd", wrot[1][:].rearrange(
                "p (c n) -> p c n", c=16)[:, :, 0:128],
                self.din["wv"].rearrange("(c p) n -> p c n", p=128)[
                    :, :, pj * 128:(pj + 1) * 128],
                waits=(w_war[1],), sem=dwr[1])
            tvB = edma("gpsimd", wrot[1][:].rearrange(
                "p (c n) -> p c n", c=16)[:, :, 128:256],
                self.din["wv"].rearrange("(c p) n -> p c n", p=128)[
                    :, :, (8 + pj) * 128:(9 + pj) * 128], sem=dwr[1])
            lastk = None
            for hx in range(2):
                for tq in range(4):
                    mmt = None
                    for ic in range(16):
                        w = [tkA, tkB, prp_war] if ic == 0 else []
                        mmt = et(lambda e, ic=ic, hx=hx, tq=tq: e.matmul(
                            prp[:, 0:512],
                            wrot[0][:, ic * 256 + hx * 128:
                                     ic * 256 + (hx + 1) * 128],
                            xnT_c(ic, slice(tq * 512, (tq + 1) * 512)),
                            start=(ic == 0), stop=(ic == 15)),
                            waits=w, inc=(ic == 15))
                    lastk = mmt
                    prp_war = ev(lambda e, hx=hx, tq=tq: e.tensor_copy(
                        kbf[hx][:, tq * 512:(tq + 1) * 512], prp[:, 0:512]),
                        waits=(mmt,))
            w_war[0] = lastk
            lastv = None
            for tv in range(16):
                mmt = None
                for ic in range(16):
                    w = [tvA, tvB, prp_war] if ic == 0 else []
                    mmt = et(lambda e, ic=ic, tv=tv: e.matmul(
                        prp[:, 0:256],
                        xnT_c(ic, slice(tv * 128, (tv + 1) * 128)),
                        wrot[1][:, ic * 256:(ic + 1) * 256],
                        start=(ic == 0), stop=(ic == 15)),
                        waits=w, inc=(ic == 15))
                lastv = mmt
                ev(lambda e, tv=tv: e.tensor_copy(
                    vbf[0][:, tv * 128:(tv + 1) * 128], prp[:, 0:128]),
                    waits=(mmt,), inc=False)
                prp_war = ev(lambda e, tv=tv: e.tensor_copy(
                    vbf[1][:, tv * 128:(tv + 1) * 128], prp[:, 128:256]))
            w_war[1] = lastv
            pair_gate = prp_war

            for _ in range(4):
                emit_mean_step()

            from collections import deque
            pending = deque()
            for nseg, (t, sig, s) in enumerate(SEGLIST):
                i = gseg[0]
                ext = emit_QK(t, sig, s, i,
                              gate=pair_gate if nseg == 0 else None)
                is_end = (sig == 3 and s == NKB[t] - 1)
                pending.append((t, sig, s, i, ext, is_end))
                gseg[0] += 1
                if len(pending) >= 3:
                    (pt, psig, ps, pi, pext, pend) = pending.popleft()
                    yev = emit_TRPV(pt, psig, ps, pi, pext)
                    if pend:
                        emit_combine(pj, pt, yev)
                        for _ in range(3):
                            emit_mean_step()
            while pending:
                (pt, psig, ps, pi, pext, pend) = pending.popleft()
                yev = emit_TRPV(pt, psig, ps, pi, pext)
                if pend:
                    emit_combine(pj, pt, yev)

        while mst["i"] < len(mean_srcs):
            emit_mean_step()

        p1a.close()
        p1w = _Scope()
        s1w = p1w.stack(nc)
        wob = [self.sb(f"wob{i}", [128, 16 * 512], BF16, s1w)
               for i in range(2)]
        xpt = [self.sb(f"xpt{i}", [128, 512], F32, s1w) for i in range(2)]
        pot = [self.sb(f"pot{i}", [128, 512], F32, s1w) for i in range(2)]
        wop = self.psb("wop", [128, 512], F32, s1w)
        att_v = self.vgap()
        att_a = self.agap()
        att_t = self.P.emit("tensor", lambda e: e.matmul(
            wop[:, 0:16], self.xnTq[:, 0:128], self.xnTq[:, 0:16],
            start=True, stop=True), waits=(), inc=self.sT)
        ATT = (prev_combine, att_v, att_a, att_t)
        wob_war = [ATT, ATT]
        pot_war = [ATT, ATT]
        xpt_war = [ATT, ATT]
        wop_war = att_t
        last_part = None
        for db in range(4):
            ww = wob_war[db % 2]
            ww = ww if isinstance(ww, tuple) and len(ww) and isinstance(
                ww[0], tuple) else (ww,)
            dwo = edma("gpsimd", wob[db % 2][:].rearrange(
                "p (c n) -> p c n", c=16),
                self.din["wo"].rearrange("(c p) n -> p c n", p=128)[
                    :, :, db * 512:(db + 1) * 512],
                waits=ww, sem=dwob[db % 2])
            for tt in range(8):
                i = db * 8 + tt
                mmt = None
                for ic in range(16):
                    w = [dwo, wop_war, prev_combine] if ic == 0 else []
                    mmt = et(lambda e, ic=ic, tt=tt, db=db: e.matmul(
                        wop[:, 0:512],
                        self.yT[:, ic * 1024 + tt * 128:
                                ic * 1024 + (tt + 1) * 128],
                        wob[db % 2][:, ic * 512:(ic + 1) * 512],
                        start=(ic == 0), stop=(ic == 15)),
                        waits=w, inc=(ic == 15))
                wob_war[db % 2] = mmt
                xw = xpt_war[i % 2]
                xw = xw if isinstance(xw, tuple) and len(xw) and isinstance(
                    xw[0], tuple) else (xw,)
                dxp = edma("sync", xpt[i % 2][:],
                           self.din["xq"][tt * 128:(tt + 1) * 128,
                                          db * 512:(db + 1) * 512],
                           waits=xw, sem=dxpt[i % 2])
                pw = pot_war[i % 2]
                pw = pw if isinstance(pw, tuple) and len(pw) and isinstance(
                    pw[0], tuple) else (pw,)
                ptok = ev(lambda e, i=i: e.tensor_add(
                    pot[i % 2][:], wop[:, 0:512], xpt[i % 2][:]),
                    waits=(mmt, dxp) + pw)
                wop_war = ptok
                xpt_war[i % 2] = ptok
                last_part = edma("sync", self.d_spart[
                    tt * 128:(tt + 1) * 128, db * 512:(db + 1) * 512],
                    pot[i % 2][:], waits=(ptok,), sem=dspt)
                pot_war[i % 2] = last_part

        fin = []
        for acc, sw, mn in ((macc1, self.sw1, self.mean1),
                            (macc2, self.sw2, self.mean2)):
            ev(lambda e, acc=acc: e.reduce_sum(
                msc[:, 0:1], acc[:, :], AX.X),
                waits=(mst["done"][-1],), inc=False)
            vgap()
            g1 = vgap()
            mm1 = et(lambda e: e.matmul(
                wop[0:1, 0:1], self.onesc[:], msc[:, 0:1],
                start=True, stop=True), waits=(g1, wop_war))
            cv = ev(lambda e: e.tensor_scalar(
                msc[0:1, 1:2], wop[0:1, 0:1],
                1.0 / (D * INNER), QEPS, ALU.mult, ALU.max), waits=(mm1,))
            vgap()
            g2 = vgap()
            mm2 = et(lambda e: e.matmul(
                wop[:, 4:5], self.onesr[:], msc[0:1, 1:2],
                start=True, stop=True), waits=(g2,))
            cb = ev(lambda e, mn=mn: e.tensor_copy(mn, wop[:, 4:5]),
                    waits=(mm2,))
            vgap()
            rc_ = ev(lambda e, sw=sw, mn=mn: e.reciprocal(sw, mn),
                     waits=(cb,))
            vgap()
            g3 = vgap()
            wop_war = rc_
            fin.append(g3)

        self.vgap(waits=(last_part,))
        f_v = self.vgap()
        self.agap()
        f_a = self.agap()
        f_t = et(lambda e: e.transpose(trp[0][:, 0:128], ident[:],
                                        ident[:]), waits=tuple(trp_war))
        f_p = self.pgap()
        f_sy = self.P.emit("sync", lambda e: e.nop(),
                           waits=self.dma_fence_waits(), inc=self.sP)
        self.P1F = (f_v, f_a, f_t, f_p, f_sy, fin[0], fin[1])
        p1w.close()
        p1o.close()

    def phase2(self):
        nc = self.nc
        P1F = self.P1F
        ev, ea, et, ep, edma = self.ev, self.ea, self.et, self.ep, self.edma
        vgap, agap, pgap = self.vgap, self.agap, self.pgap
        p2 = _Scope()
        s2 = p2.stack(nc)
        h = self.sb("hbuf", [128, 8 * INNER], F16, s2)

        nsw1 = self.slot()
        nsw2 = self.slot()
        bm05 = self.slot()
        bp05 = self.slot()
        bbig = self.slot()
        ev(lambda e: e.memset(bm05, -0.5), inc=False)
        ev(lambda e: e.memset(bp05, 0.5), inc=False)
        ev(lambda e: e.memset(bbig, BIGC + 0.5), inc=False)
        hq_ssq = self.slot(8)
        hq_ma = self.slot(8)
        hq_n = self.slot(8)
        hq_rn = self.slot(8)
        hq_macl = self.slot(8)
        hq_tsc = self.slot(8)
        ev(lambda e: e.tensor_scalar(
            self.g1v, self.fq_sxo, self.mean1, 1.0 / 127.0, ALU.mult,
            ALU.mult), waits=P1F, inc=False)
        ev(lambda e: e.tensor_scalar_mul(nsw1, self.sw1, -1.0), inc=False)
        ev(lambda e: e.tensor_scalar_mul(nsw2, self.sw2, -1.0), inc=False)
        vgap()
        scgate = vgap()

        p2a = _Scope()
        s2a = p2a.stack(nc)
        xqTs = self.sb("xqTs", [128, 16 * 1024], BF16, s2a)
        NW1 = 4
        w1rot = [self.sb(f"w1r{i}", [128, 512], F32, s2a)
                 for i in range(NW1)]
        wqs = [self.sb(f"wqs{i}", [128, 16 * 512], BF16, s2a)
               for i in range(2)]
        gtmp = self.sb("gtmp", [128, 512], F32, s2a)
        pq = [self.sb(f"pq{i}", [128, 512], BF16, s2a) for i in range(4)]
        ps1 = [self.psb(f"ps1_{i}", [128, 512], F32, s2a) for i in range(8)]

        dxq2 = self.mkdsem("dxq2")
        dw1 = [self.mkdsem(f"dw1{i}") for i in range(4)]
        dw2 = [self.mkdsem(f"dw2{i}") for i in range(4)]
        dxp2 = [self.mkdsem("dxp2a"), self.mkdsem("dxp2b")]
        dout = self.mkdsem("dout")
        txq = edma("sync", xqTs[:].rearrange("p (c n) -> p c n", c=16),
                   self.d_sxqt.rearrange("(c p) n -> p c n", p=128),
                   waits=P1F, sem=dxq2)
        w1r_war = [scgate] * NW1
        pq_war = [scgate, scgate]
        wq_war = [None, None]
        ps1_free = [None] * 8
        gt_war = None
        for ob in range(16):
            wq_tok = None
            for ic in range(16):
                dtok = edma("sync", w1rot[ic % NW1][:],
                            self.din["w1t"][ic * 128:(ic + 1) * 128,
                                            ob * 512:(ob + 1) * 512],
                            waits=(w1r_war[ic % NW1],), sem=dw1[ic % NW1])
                w = [dtok, scgate, pq_war[ic % 2]]
                if ic == 0:
                    w.append(wq_war[ob % 2])
                pp0, pp1 = pq[2 * (ic % 2)], pq[2 * (ic % 2) + 1]
                ev(lambda e, pp0=pp0, ic=ic: e.tensor_scalar(
                    pp0[:], w1rot[ic % NW1][:], self.sw1, 0.5, ALU.mult,
                    ALU.is_gt), waits=w, inc=False)
                cpn = ev(lambda e, pp1=pp1, ic=ic: e.tensor_scalar(
                    pp1[:], w1rot[ic % NW1][:], nsw1, 0.5, ALU.mult,
                    ALU.is_gt))
                w1r_war[ic % NW1] = cpn
                ev(lambda e, pp0=pp0, pp1=pp1, ic=ic, ob=ob:
                   e.tensor_sub(
                    wqs[ob % 2][:, ic * 512:(ic + 1) * 512], pp0[:],
                    pp1[:]), inc=False)
                wq_tok = vgap()
                pq_war[ic % 2] = wq_tok
            mm_tok = [None] * 8
            for tt in range(8):
                mmt = None
                for ic in range(16):
                    w = [wq_tok, ps1_free[tt], txq] if ic == 0 else []
                    mmt = et(lambda e, ic=ic, tt=tt, ob=ob: e.matmul(
                        ps1[tt][:, 0:512],
                        xqTs[:, ic * 1024 + tt * 128:
                             ic * 1024 + (tt + 1) * 128],
                        wqs[ob % 2][:, ic * 512:(ic + 1) * 512],
                        start=(ic == 0), stop=(ic == 15)),
                        waits=w, inc=(ic == 15))
                mm_tok[tt] = mmt
            wq_war[ob % 2] = mm_tok[7]
            for tt in range(8):
                a1 = ea(lambda e, tt=tt: e.activation(
                    gtmp[:], ps1[tt][:, 0:512], AF.Gelu,
                    scale=self.g1v[:, tt:tt + 1]),
                    waits=(mm_tok[tt], gt_war))
                ps1_free[tt] = a1
                a2 = ea(lambda e, tt=tt, ob=ob: e.activation(
                    h[:, tt * INNER + ob * 512:tt * INNER + (ob + 1) * 512],
                    gtmp[:], AF.Gelu))
                gt_war = a2
        ffn1_act = gt_war
        p2a.close()
        if self.upto == 3:
            p2.close()
            self._stub_out(ffn1_act)
            return

        p2b = _Scope()
        s2b = p2b.stack(nc)
        hqs = self.sb("hqs", [128, INNER], F16, s2b)
        idf16 = self.sb("idf16", [128, 128], F16, s2b)
        ev(lambda e: e.tensor_copy(idf16[:], self.ident[:]),
           waits=(ffn1_act,), inc=False)
        vgap()
        idf_tok = vgap()
        qu = self.sb("qu2", [128, 2048], F32, s2b)
        qui = self.sb("qui2", [128, 2048], I32, s2b)
        qc1 = self.sb("qc12", [128, 2048], F32, s2b)
        trp2 = self.psb("trp2", [128, 1024], F16, s2b)

        def hsl(tt, a, b):
            return h[:, tt * INNER + a:tt * INNER + b]

        hq_done = []
        trp2_war = None
        hqs_war = None
        for tt in range(8):
            sq = ea(lambda e, tt=tt: e.activation(
                hqs[:], hsl(tt, 0, INNER), AF.Square,
                accum_out=hq_ssq[:, tt:tt + 1]),
                waits=(ffn1_act, hqs_war))
            mx = ev(lambda e, tt=tt: e.reduce_max(
                hq_ma[:, tt:tt + 1], hsl(tt, 0, INNER), AX.X,
                apply_absolute_value=True), waits=(sq,))
            agap()
            sqa = agap()
            ea(lambda e, tt=tt: e.activation(
                hq_n[:, tt:tt + 1], hq_ssq[:, tt:tt + 1], AF.Sqrt),
                waits=(sqa,), inc=False)
            na = agap()
            ev(lambda e, tt=tt: e.tensor_scalar_max(
                hq_n[:, tt:tt + 1], hq_n[:, tt:tt + 1], 1e-12),
                waits=(na, mx), inc=False)
            vgap()
            ev(lambda e, tt=tt: e.reciprocal(
                hq_rn[:, tt:tt + 1], hq_n[:, tt:tt + 1]), inc=False)
            vgap()
            ev(lambda e, tt=tt: e.tensor_tensor(
                hq_macl[:, tt:tt + 1], hq_ma[:, tt:tt + 1],
                hq_rn[:, tt:tt + 1], ALU.mult), inc=False)
            vgap()
            ev(lambda e, tt=tt: e.tensor_scalar(
                hq_macl[:, tt:tt + 1], hq_macl[:, tt:tt + 1],
                math.sqrt(INNER), QEPS, ALU.mult, ALU.max), inc=False)
            vgap()
            ev(lambda e, tt=tt: e.tensor_scalar(
                self.hsv[:, tt:tt + 1], hq_macl[:, tt:tt + 1],
                self.mean2, 1.0 / 127.0, ALU.mult, ALU.mult), inc=False)
            vgap()
            ev(lambda e, tt=tt: e.reciprocal(
                hq_tsc[:, tt:tt + 1], hq_macl[:, tt:tt + 1]), inc=False)
            vgap()
            ev(lambda e, tt=tt: e.tensor_tensor(
                hq_tsc[:, tt:tt + 1], hq_tsc[:, tt:tt + 1],
                hq_rn[:, tt:tt + 1], ALU.mult), inc=False)
            vgap()
            ev(lambda e, tt=tt: e.tensor_scalar_mul(
                hq_tsc[:, tt:tt + 1], hq_tsc[:, tt:tt + 1],
                127.0 * math.sqrt(INNER)), inc=False)
            vgap()
            tscg = vgap()
            lastq = None
            for qq in range(4):
                a0, b0 = qq * 2048, (qq + 1) * 2048
                pu = ea(lambda e, tt=tt, a0=a0, b0=b0: e.activation(
                    qu[:], hsl(tt, a0, b0), AF.Identity,
                    bias=bbig, scale=hq_tsc[:, tt:tt + 1]),
                    waits=(tscg, lastq))
                ev(lambda e: e.tensor_copy(qui[:], qu[:]), waits=(pu,),
                   inc=False)
                ev(lambda e: e.tensor_copy(qc1[:], qui[:]), inc=False)
                ev(lambda e: e.tensor_tensor(qu[:], qc1[:], qu[:],
                                             ALU.is_gt), inc=False)
                lastq = ev(lambda e, a0=a0, b0=b0: e.scalar_tensor_tensor(
                    hqs[:, a0:b0], qc1[:], -BIGC, qu[:], ALU.add,
                    ALU.subtract))
            hw = None
            for g in range(8):
                trt = None
                for k in range(8):
                    oc = g * 8 + k
                    w = [lastq, trp2_war] if k == 0 else []
                    trt = et(lambda e, oc=oc, k=k: e.transpose(
                        trp2[:, k * 128:(k + 1) * 128],
                        hqs[:, oc * 128:(oc + 1) * 128], idf16[:]),
                        waits=w + [idf_tok] if k == 0 else [], inc=(k == 7))
                hw = ev(lambda e, tt=tt, g=g: e.tensor_copy(
                    hsl(tt, g * 1024, (g + 1) * 1024), trp2[:, 0:1024]),
                    waits=(trt,))
                trp2_war = hw
            hqs_war = hw
            hq_done.append(hw)
        p2b.close()
        if self.upto == 4:
            p2.close()
            self._stub_out(hq_done[-1])
            return

        p2c = _Scope()
        s2c = p2c.stack(nc)
        NW2 = 4
        w2rot = [self.sb(f"w2r{i}", [128, 512], F32, s2c)
                 for i in range(NW2)]
        w2q = [self.sb(f"w2q{i}", [128, 512], F16, s2c)
               for i in range(NW2)]
        pq2 = [self.sb(f"pq2_{i}", [128, 512], BF16, s2c) for i in range(4)]
        xpt2 = [self.sb(f"xpt2_{i}", [128, 512], F32, s2c) for i in range(2)]
        pot2 = [self.sb(f"pot2_{i}", [128, 512], F32, s2c) for i in range(2)]
        ps2 = [self.psb(f"ps2_{i}", [128, 512], F32, s2c) for i in range(8)]

        NDB = 1 if self.upto == 5 else (2 if self.upto == 6 else 4)
        hq_gate = hq_done[-1]
        w2r_war = [hq_gate] * NW2
        pq2_war = [hq_gate, hq_gate]
        w2q_war = [hq_gate] * NW2
        ps2_free = [hq_gate] * 8
        xpt2_war = [hq_gate, hq_gate]
        pot2_war = [hq_gate, hq_gate]
        for db in range(NDB):
            mm_last = [None] * 8
            for oc in range(64):
                dtok = edma("sync", w2rot[oc % NW2][:],
                            self.din["w2t"][oc * 128:(oc + 1) * 128,
                                            db * 512:(db + 1) * 512],
                            waits=(w2r_war[oc % NW2],), sem=dw2[oc % NW2])
                w = [dtok, scgate, w2q_war[oc % NW2], pq2_war[oc % 2]]
                qp0, qp1 = pq2[2 * (oc % 2)], pq2[2 * (oc % 2) + 1]
                ev(lambda e, qp0=qp0, oc=oc: e.tensor_scalar(
                    qp0[:], w2rot[oc % NW2][:], self.sw2, 0.5, ALU.mult,
                    ALU.is_gt), waits=w, inc=False)
                cpn = ev(lambda e, qp1=qp1, oc=oc: e.tensor_scalar(
                    qp1[:], w2rot[oc % NW2][:], nsw2, 0.5, ALU.mult,
                    ALU.is_gt))
                w2r_war[oc % NW2] = cpn
                ev(lambda e, qp0=qp0, qp1=qp1, oc=oc:
                   e.tensor_sub(
                    w2q[oc % NW2][:], qp0[:], qp1[:]), inc=False)
                wq2_tok = vgap()
                pq2_war[oc % 2] = wq2_tok
                mmt = None
                for tt in range(8):
                    w = [wq2_tok, hq_gate]
                    if oc == 0:
                        w.append(ps2_free[tt])
                    mmt = et(lambda e, oc=oc, tt=tt, db=db: e.matmul(
                        ps2[tt][:, 0:512],
                        h[:, tt * INNER + oc * 128:
                          tt * INNER + (oc + 1) * 128],
                        w2q[oc % NW2][:], start=(oc == 0), stop=(oc == 63)),
                        waits=w if tt == 0 else ([ps2_free[tt]]
                                                 if oc == 0 else []),
                        inc=(oc == 63 or tt == 7))
                    mm_last[tt] = mmt
                w2q_war[oc % NW2] = mmt
            for tt in range(8):
                i = db * 8 + tt
                dxp = edma("sync", xpt2[i % 2][:],
                           self.d_spart[tt * 128:(tt + 1) * 128,
                                        db * 512:(db + 1) * 512],
                           waits=(xpt2_war[i % 2],), sem=dxp2[i % 2])
                ptk = ev(lambda e, i=i, tt=tt: e.scalar_tensor_tensor(
                    pot2[i % 2][:], ps2[tt][:, 0:512],
                    self.hsv[:, tt:tt + 1], xpt2[i % 2][:],
                    ALU.mult, ALU.add),
                    waits=(mm_last[tt], dxp, pot2_war[i % 2]))
                ps2_free[tt] = ptk
                xpt2_war[i % 2] = ptk
                otok = edma("sync", self.d_out[
                    tt * 128:(tt + 1) * 128, db * 512:(db + 1) * 512],
                    pot2[i % 2][:], waits=(ptk,), sem=dout)
                pot2_war[i % 2] = otok
        if self.upto in (5, 6):
            zb = self.sb("zst3", [128, D - 512 * NDB], F32, s2c)
            tz = self.ev(lambda e: e.memset(zb[:], 0.0),
                         waits=(pot2_war[0], pot2_war[1]))
            for tt in range(8):
                self.edma("sync",
                          self.d_out[tt * 128:(tt + 1) * 128, 512 * NDB:],
                          zb[:], waits=(tz,))
        self.P.emit("sync", lambda e: e.nop(),
                    waits=self.dma_fence_waits(), inc=self.sP)
        p2c.close()
        p2.close()

    def _stub_out(self, gate):
        zb = self.sb("zstub2", [128, D], F32)
        fv = self.vgap(waits=(gate,))
        fa = self.agap()
        fp = self.pgap()
        tz = self.ev(lambda e: e.memset(zb[:], 0.0),
                     waits=(fv, fa, fp))
        for tt in range(8):
            self.edma("sync", self.d_out[tt * 128:(tt + 1) * 128, :],
                      zb[:], waits=(tz,))
        self.P.emit("sync", lambda e: e.nop(),
                    waits=self.dma_fence_waits(), inc=self.sP)

    def emit_taps(self):
        tapdefs = {
            "xnT": (self.xnT[:], [128, 16 * T], BF16),
            "xnTq": (self.xnTq[:], [128, 16 * 1024], BF16),
            "fq_sxo": (self.fq_sxo, [128, 8], F32),
            "yT": (self.yT[:], [128, 16 * 1024], BF16),
            "lam": (self.lam[:], [128, NQ * H], F32),
            "sxo2": (self.smp[:, 0:64], [128, 64], F32),
        }
        for name in self.taps:
            ap, shape, dt_ = tapdefs[name]
            dram = self.mktap(name, shape, dt_)
            self.edma("sync", dram, ap, waits=self.P1F)
        self.P.emit("sync", lambda e: e.nop(),
                    waits=((self.dSY, self.dSY.n),), inc=self.sP)


class _Scope:

    def __init__(self):
        self._st = None

    def stack(self, nc):
        if self._st is None:
            from contextlib import ExitStack
            self._st = ExitStack()
        return self._st

    def close(self):
        if self._st is not None:
            self._st.close()
            self._st = None

    def __enter__(self):
        return self

    def __exit__(self, *a):
        self.close()
        return False


def build_nc(taps=(), upto=2):
    from contextlib import ExitStack
    nc = bass.Bass("TRN2", target_bir_lowering=False,
                   detect_race_conditions=False)
    with ExitStack() as st:
        b = Builder(nc, st, taps, upto)
        b.build()
    return nc, sorted(b.d_taps)


def make_masks(r):
    m = np.zeros((NQ, 128, 512), dtype=np.float32)
    jj = np.arange(512)[None, :]
    pp = np.arange(128)[:, None]
    for t in range(NQ):
        g = 2 * t + r
        delta = g * 128 - (NKB[t] - 1) * 512
        m[t] = np.where(jj <= delta + pp, 0.0, -1e9)
    return m.reshape(NQ * 128, 512).astype(bf16)


def shard_inputs(inputs):
    x = np.asarray(inputs["L0_attn_x"])
    wq1 = np.ascontiguousarray(np.asarray(inputs["L0_attn_Wq1"]))
    wq2 = np.ascontiguousarray(np.asarray(inputs["L0_attn_Wq2"]))
    wk = np.ascontiguousarray(np.asarray(inputs["L0_attn_Wk"]))
    wv = np.ascontiguousarray(np.asarray(inputs["L0_attn_Wv"]))
    wlam = np.ascontiguousarray(np.asarray(inputs["L0_attn_Wlam"]))
    wo = np.ascontiguousarray(np.asarray(inputs["L0_attn_Wo"]))
    w1t = np.ascontiguousarray(np.asarray(inputs["L0_ffn_w1"]).T)
    w2t = np.ascontiguousarray(np.asarray(inputs["L0_ffn_w2"]).T)
    ident = np.eye(128, dtype=bf16)
    in_maps = []
    for c in range(8):
        b, r = c >> 1, c & 1
        rows = np.concatenate([np.arange((2 * t + r) * 128,
                                         (2 * t + r + 1) * 128)
                               for t in range(NQ)])
        in_maps.append({
            "xf": np.ascontiguousarray(x[b]),
            "xq": np.ascontiguousarray(x[b][rows]),
            "wq1": wq1, "wq2": wq2, "wk": wk, "wv": wv, "wo": wo,
            "wlam": wlam, "w1t": w1t, "w2t": w2t,
            "ident": ident, "cmask": make_masks(r),
        })
    return in_maps


def kernel(**inputs):
    nc, _ = build_nc()
    in_maps = shard_inputs(inputs)
    res = run_bass_kernel_spmd(nc, in_maps, core_ids=list(range(8)))
    x = np.asarray(inputs["L0_attn_x"])
    out = np.empty((B, T, D), dtype=np.float32)
    for c in range(8):
        b, r = c >> 1, c & 1
        o = np.asarray(res.results[c]["out"])
        for t in range(NQ):
            out[b, (2 * t + r) * 128:(2 * t + r + 1) * 128, :] = \
                o[t * 128:(t + 1) * 128, :]
    return out


# version 22
# speedup vs baseline: 1.0314x; 1.0314x over previous
import math
import numpy as np
import ml_dtypes
import concourse.bass as bass
import concourse.mybir as mybir
from concourse.bass_utils import run_bass_kernel_spmd

F32 = mybir.dt.float32
F16 = mybir.dt.float16
BF16 = mybir.dt.bfloat16
I32 = mybir.dt.int32
AF = mybir.ActivationFunctionType
ALU = mybir.AluOpType
AX = mybir.AxisListType
bf16 = ml_dtypes.bfloat16

B, T, D, H, DH, INNER = 4, 2048, 2048, 16, 128, 8192
NQ = 8
NKB = [1, 1, 2, 2, 3, 3, 4, 4]
SEGS = [1, 1, 1, 1, 2, 2, 2, 2]
ISQD = 1.0 / math.sqrt(DH)
BIGC = 384.0
LNEPS = 1e-5
QEPS = 1e-5


class Sem:
    def __init__(self, h, name):
        self.h = h
        self.n = 0
        self.name = name

    def inc(self, k=1):
        self.n += k
        return self.n


class Prog:
    ENGS = ("sync", "gpsimd", "vector", "scalar", "tensor")

    def __init__(self):
        self.ops = {e: [] for e in self.ENGS}

    def emit(self, eng, fn, waits=(), inc=None, inc_k=1):
        tok = None
        incs = []
        if inc is not None:
            v = inc.inc(inc_k)
            incs = [(inc, inc_k)]
            tok = (inc, v)
        self.ops[eng].append((tuple(w for w in waits if w is not None),
                              fn, incs))
        return tok

    def check_deadlock(self):
        pcs = {e: 0 for e in self.ENGS}
        cnt = {}
        total = sum(len(v) for v in self.ops.values())
        done = 0
        progress = True
        while progress:
            progress = False
            for e in self.ENGS:
                while pcs[e] < len(self.ops[e]):
                    waits, _, incs = self.ops[e][pcs[e]]
                    if all(cnt.get(id(s), 0) >= v for s, v in waits):
                        for s, k in incs:
                            cnt[id(s)] = cnt.get(id(s), 0) + k
                        pcs[e] += 1
                        done += 1
                        progress = True
                    else:
                        break
        if done != total:
            detail = []
            for e in self.ENGS:
                if pcs[e] < len(self.ops[e]):
                    waits = self.ops[e][pcs[e]][0]
                    detail.append(
                        f"{e}@{pcs[e]}/{len(self.ops[e])}: waiting "
                        + str([(s.name, v, cnt.get(id(s), 0))
                               for s, v in waits]))
            raise RuntimeError("build-time deadlock:\n" + "\n".join(detail))

    def run_bodies(self, block):
        def body(name):
            def f(eng):
                for waits, fn, incs in self.ops[name]:
                    for s, v in waits:
                        eng.wait_ge(s.h, v)
                    inst = fn(eng)
                    for s, k in incs:
                        inst.then_inc(s.h, k)
            return f
        block.sync(body("sync"))
        block.gpsimd(body("gpsimd"))
        block.vector(body("vector"))
        block.scalar(body("scalar"))
        block.tensor(body("tensor"))


class Builder:
    def __init__(self, nc, st, taps, upto=2):
        self.nc = nc
        self.st = st
        self.taps = taps
        self.upto = upto
        self.P = Prog()
        self.d_taps = {}
        self.din = {}
        for nm, shp in [("xf", [T, D]), ("xq", [1024, D]), ("wq1", [D, D]),
                        ("wq2", [D, D]), ("wk", [D, D]), ("wv", [D, D]),
                        ("wo", [D, D]), ("wlam", [D, H]),
                        ("w1t", [D, INNER]), ("w2t", [INNER, D])]:
            self.din[nm] = nc.dram_tensor(nm, shp, F32,
                                          kind="ExternalInput").ap()
        self.d_id = nc.dram_tensor("ident", [128, 128], BF16,
                                   kind="ExternalInput").ap()
        self.d_cm = nc.dram_tensor("cmask", [NQ * 128, 512], BF16,
                                   kind="ExternalInput").ap()
        self.d_out = nc.dram_tensor("out", [1024, D], F32,
                                    kind="ExternalOutput").ap()
        self.d_sxqt = nc.dram_tensor("sp_xqt", [D, 1024], BF16,
                                     kind="Internal").ap()
        self.d_spart = nc.dram_tensor("sp_part", [1024, D], F32,
                                      kind="Internal").ap()

    def sb(self, n, s, d, stack=None):
        return (stack or self.st).enter_context(self.nc.sbuf_tensor(n, s, d))

    def psb(self, n, s, d, stack=None):
        return (stack or self.st).enter_context(self.nc.psum_tensor(n, s, d))

    def mktap(self, name, shape, dtype):
        if name in self.taps:
            ap = self.nc.dram_tensor("tap_" + name, shape, dtype,
                                     kind="ExternalOutput").ap()
            self.d_taps[name] = ap
            return ap
        return None

    def setup_sems(self):
        mk = lambda n: Sem(self.st.enter_context(self.nc.semaphore(n)), n)
        self.all_dma_sems = []
        self.dSY, self.dGP, self.dSC = mk("dSY"), mk("dGP"), mk("dSC")
        self.all_dma_sems += [self.dSY, self.dGP, self.dSC]
        self.sV, self.sA, self.sT, self.sP = (mk("sV"), mk("sA"), mk("sT"),
                                              mk("sP"))

    def dma_fence_waits(self):
        return tuple((s, s.n) for s in self.all_dma_sems)

    def ev(self, fn, waits=(), inc=True):
        return self.P.emit("vector", fn, waits, self.sV if inc else None)

    def ea(self, fn, waits=(), inc=True):
        return self.P.emit("scalar", fn, waits, self.sA if inc else None)

    def et(self, fn, waits=(), inc=True):
        return self.P.emit("tensor", fn, waits, self.sT if inc else None)

    def ep(self, fn, waits=(), inc=True):
        return self.P.emit("gpsimd", fn, waits, self.sP if inc else None)

    def edma(self, q, out, in_, waits=(), sem=None):
        if sem is None:
            sem = {"sync": self.dSY, "gpsimd": self.dGP,
                   "scalar": self.dSC}[q]
        return self.P.emit(q, (lambda o, i: lambda eng: eng.dma_start(
            out=o, in_=i))(out, in_), waits, inc=sem, inc_k=16)

    def mkdsem(self, n):
        s = Sem(self.st.enter_context(self.nc.semaphore(n)), n)
        self.all_dma_sems.append(s)
        return s

    def vgap(self, waits=()):
        j = self.junk1
        return self.ev(lambda e: e.tensor_scalar_mul(
            j[:, 0:1], j[:, 1:2], 1.0), waits)

    def agap(self, waits=()):
        j = self.junk1
        return self.ea(lambda e: e.copy(j[:, 2:3], j[:, 3:4]), waits)

    def pgap(self, waits=()):
        j = self.junk1
        return self.ep(lambda e: e.tensor_scalar(
            j[:, 4:5], j[:, 5:6], 1.0, None, ALU.mult), waits)

    def slot(self, n=1):
        a = self._sl
        self._sl += n
        assert self._sl <= self.NSM, "small pool overflow"
        return self.smp[:, a:a + n]

    def build(self):
        self.setup_sems()
        nc = self.nc
        self.junk1 = self.sb("junk1", [128, 8], F32)
        self.ident = self.sb("identsb", [128, 128], BF16)
        self.onesc = self.sb("onesc", [128, 1], F32)
        self.onesr = self.sb("onesr", [1, 128], F32)
        self.NSM = 264
        self.smp = self.sb("smp", [128, self.NSM], F32)
        self._sl = 0
        self.epsb = self.slot()
        self.fq_sxo = self.slot(8)
        self.g1v = self.slot(8)
        self.hsv = self.slot(8)
        self.mw1p = self.slot(16)
        self.mw2p = self.slot(16)
        self.sw1 = self.slot()
        self.sw2 = self.slot()
        self.mean1 = self.slot()
        self.mean2 = self.slot()

        self.s01 = _Scope()
        s01 = self.s01.stack(self.nc)
        self.xnT = self.sb("xnT", [128, 16 * T], BF16, s01)
        self.xnTq = self.sb("xnTq", [128, 16 * 1024], BF16, s01)
        self.yT = self.sb("yT", [128, 16 * 1024], BF16, s01)
        self.lam = self.sb("lam", [128, NQ * H], F32, s01)
        self.masks = self.sb("maskssb", [128, NQ * 512], BF16, s01)

        self.tok_id = self.edma("sync", self.ident[:], self.d_id)
        self.tok_msk = self.edma(
            "sync", self.masks[:].rearrange("p (t n) -> p t n", t=NQ),
            self.d_cm.rearrange("(t p) n -> p t n", p=128))
        self.ev(lambda e: e.memset(self.onesc[:], 1.0), inc=False)
        self.ev(lambda e: e.memset(self.onesr[:], 1.0), inc=False)
        self.ev(lambda e: e.memset(self.junk1[:], 1.0), inc=False)
        self.ev(lambda e: e.memset(self.epsb, LNEPS), inc=False)
        self.vgap()
        self.CONST = self.vgap()

        self.phase0()
        if self.upto >= 1:
            self.phase1()
        else:
            self.P1F = self.P0F
        self.emit_taps()
        self.s01.close()
        if self.upto >= 2:
            self.phase2()
        else:
            zb = self.sb("zstub", [128, D], F32)
            tz = self.ev(lambda e: e.memset(zb[:], 0.0), waits=self.P1F)
            for tt in range(8):
                self.edma("sync", self.d_out[tt * 128:(tt + 1) * 128, :],
                          zb[:], waits=(tz,))
            self.P.emit("sync", lambda e: e.nop(),
                        waits=self.dma_fence_waits(), inc=self.sP)
        self.P.check_deadlock()
        blk = self.st.enter_context(nc.Block())
        self.P.run_bodies(blk)

    def xnT_c(self, c, sl=slice(None)):
        return self.xnT[:, c * T:(c + 1) * T][:, sl]

    def xnTq_c(self, c, sl=slice(None)):
        return self.xnTq[:, c * 1024:(c + 1) * 1024][:, sl]

    def yT_c(self, c, sl=slice(None)):
        return self.yT[:, c * 1024:(c + 1) * 1024][:, sl]

    def phase0(self):
        p0 = self.st.enter_context(_Scope())
        sb0 = lambda n, s, d: self.sb(n, s, d, p0.stack(self.nc))
        xbuf = [sb0(f"xb{i}", [128, D], F32) for i in range(2)]
        sqscr = sb0("sqscr", [128, D], BF16)
        xnb = sb0("xnb", [128, D], BF16)
        xnq32 = sb0("xnq32", [128, D], F32)
        q_u = sb0("q_u", [128, 1024], F32)
        q_ui = sb0("q_ui", [128, 1024], I32)
        q_c1 = sb0("q_c1", [128, 1024], F32)
        xqb = sb0("xqb", [128, D], BF16)
        stg = sb0("stg", [128, D], BF16)
        st24 = self.slot(24)
        ptrs = [self.psb("ptr0", [128, 512], BF16, p0.stack(self.nc)),
                self.psb("ptr1", [128, 512], BF16, p0.stack(self.nc))]

        def stt_(q, k):
            return st24[:, q * 12 + k:q * 12 + k + 1]

        tiles = [("f", i) for i in range(16)] + [("q", j) for j in range(NQ)]
        dsx = [self.mkdsem("dsx0"), self.mkdsem("dsx1")]
        dstg = self.mkdsem("dstg")
        tok_consumed = {}
        ptr_war = [None, None]
        prev_xnb_free = None
        prev_q_free = None

        for idx, (kind, i) in enumerate(tiles):
            q = idx % 2
            src = (self.din["xf"] if kind == "f" else self.din["xq"])[
                i * 128:(i + 1) * 128, :]
            dtok = self.edma("sync", xbuf[q][:], src,
                             waits=(tok_consumed.get(idx - 2),),
                             sem=dsx[q])
            xb = xbuf[q]
            tA = self.ea(lambda e, xb=xb, q=q: e.activation(
                sqscr[:], xb[:], AF.Square, accum_out=stt_(q, 1)),
                waits=(dtok, prev_xnb_free))
            tV = self.ev(lambda e, xb=xb, q=q: e.reduce_sum(
                stt_(q, 0), xb[:], AX.X), waits=(dtok, tA))
            self.vgap()
            self.ev(lambda e, q=q: e.tensor_scalar_mul(
                stt_(q, 2), stt_(q, 0), -1.0 / D), inc=False)
            self.vgap()
            self.ev(lambda e, q=q: e.tensor_tensor(
                stt_(q, 11), stt_(q, 2), stt_(q, 2), ALU.mult), inc=False)
            self.ev(lambda e, q=q: e.tensor_scalar(
                stt_(q, 3), stt_(q, 1), 1.0 / D, None, ALU.mult), inc=False)
            self.vgap()
            self.ev(lambda e, q=q: e.tensor_sub(
                stt_(q, 3), stt_(q, 3), stt_(q, 11)), inc=False)
            self.vgap()
            tV2 = self.ev(lambda e, q=q: e.tensor_scalar_max(
                stt_(q, 3), stt_(q, 3), 0.0))
            self.vgap()
            self.vgap()
            self.ea(lambda e, q=q: e.activation(
                stt_(q, 11), stt_(q, 3), AF.Sqrt, bias=self.epsb),
                waits=(tV2,), inc=False)
            tA3 = self.ea(lambda e, q=q: e.activation(
                stt_(q, 6), stt_(q, 3), AF.Sqrt))
            self.agap()
            self.ev(lambda e, q=q: e.reciprocal(stt_(q, 4), stt_(q, 11)),
                    waits=(tA3,), inc=False)
            self.vgap()
            self.ev(lambda e, q=q: e.tensor_tensor(
                stt_(q, 5), stt_(q, 2), stt_(q, 4), ALU.mult), inc=False)
            self.vgap()
            tV3 = self.vgap()
            tapb = self.ea(lambda e, xb=xb, q=q: e.activation(
                xnb[:], xb[:], AF.Identity, bias=stt_(q, 5),
                scale=stt_(q, 4)), waits=(tV3, prev_xnb_free))
            tok_consumed[idx] = tapb
            tapf = None
            if kind == "q":
                tapf = self.ea(lambda e, xb=xb, q=q: e.activation(
                    xnq32[:], xb[:], AF.Identity, bias=stt_(q, 5),
                    scale=stt_(q, 4)), waits=(prev_q_free,))
                tok_consumed[idx] = tapf

            W = T if kind == "f" else 1024
            dstT = self.xnT if kind == "f" else self.xnTq
            last_tt = None
            for g in range(4):
                pp = ptrs[g % 2]
                tts = None
                for c4 in range(4):
                    c = g * 4 + c4
                    w = ([tapb, self.CONST, self.tok_id, ptr_war[g % 2]]
                         if c4 == 0 else [])
                    tts = self.et(lambda e, c=c, c4=c4, pp=pp: e.transpose(
                        pp[:, c4 * 128:(c4 + 1) * 128],
                        xnb[:, c * 128:(c + 1) * 128], self.ident[:]),
                        waits=w, inc=(c4 == 3))
                dst = dstT[:].rearrange("p (c t) -> p c t", c=16)[
                    :, g * 4:(g + 1) * 4, i * 128:(i + 1) * 128]
                tev = self.ev(lambda e, dst=dst, pp=pp: e.tensor_copy(
                    dst, pp[:, 0:512].rearrange("p (c n) -> p c n", c=4)),
                    waits=(tts,))
                ptr_war[g % 2] = tev
                last_tt = tts
            prev_xnb_free = last_tt

            if kind == "q":
                j = i
                tS = self.ev(lambda e, q=q: e.reduce_max(
                    stt_(q, 7), xnq32[:], AX.X, apply_absolute_value=True),
                    waits=(tapf,))
                self.vgap()
                self.ev(lambda e, q=q: e.tensor_tensor(
                    stt_(q, 8), stt_(q, 6), stt_(q, 4), ALU.mult), inc=False)
                self.vgap()
                self.ev(lambda e, q=q: e.tensor_scalar(
                    stt_(q, 8), stt_(q, 8), math.sqrt(D), 1e-12, ALU.mult,
                    ALU.max), inc=False)
                self.vgap()
                self.ev(lambda e, q=q: e.reciprocal(stt_(q, 10), stt_(q, 8)),
                        inc=False)
                self.vgap()
                self.ev(lambda e, q=q: e.tensor_tensor(
                    stt_(q, 9), stt_(q, 7), stt_(q, 10), ALU.mult),
                    waits=(tS,), inc=False)
                self.vgap()
                self.ev(lambda e, q=q: e.tensor_scalar(
                    stt_(q, 9), stt_(q, 9), math.sqrt(D), QEPS, ALU.mult,
                    ALU.max), inc=False)
                self.vgap()
                self.ev(lambda e, j=j, q=q: e.tensor_copy(
                    self.fq_sxo[:, j:j + 1], stt_(q, 9)), inc=False)
                self.vgap()
                self.ev(lambda e, q=q: e.reciprocal(stt_(q, 11), stt_(q, 9)),
                        inc=False)
                self.vgap()
                self.ev(lambda e, q=q: e.tensor_tensor(
                    stt_(q, 11), stt_(q, 11), stt_(q, 10), ALU.mult),
                    inc=False)
                self.vgap()
                tSc = self.ev(lambda e, q=q: e.tensor_scalar_mul(
                    stt_(q, 11), stt_(q, 11), 127.0 * math.sqrt(D)))
                self.vgap()
                self.vgap()
                tQ = None
                for hh in range(2):
                    hsl = slice(hh * 1024, (hh + 1) * 1024)
                    self.ev(lambda e, q=q, hsl=hsl: e.tensor_scalar(
                        q_u[:], xnq32[:, hsl], stt_(q, 11), BIGC + 0.5,
                        ALU.mult, ALU.add),
                        waits=(tSc, prev_q_free) if hh == 0 else (),
                        inc=False)
                    self.ev(lambda e: e.tensor_copy(q_ui[:], q_u[:]),
                            inc=False)
                    self.ev(lambda e: e.tensor_copy(q_c1[:], q_ui[:]),
                            inc=False)
                    self.ev(lambda e: e.tensor_tensor(
                        q_u[:], q_c1[:], q_u[:], ALU.is_gt), inc=False)
                    tQ = self.ev(lambda e, hsl=hsl: e.scalar_tensor_tensor(
                        xqb[:, hsl], q_c1[:], -BIGC, q_u[:], ALU.add,
                        ALU.subtract))
                tstg = None
                for g in range(4):
                    pp = ptrs[g % 2]
                    tts = None
                    for c4 in range(4):
                        c = g * 4 + c4
                        w = ([tQ, prev_xnb_free, ptr_war[g % 2]]
                             if c4 == 0 else [])
                        tts = self.et(
                            lambda e, c=c, c4=c4, pp=pp: e.transpose(
                                pp[:, c4 * 128:(c4 + 1) * 128],
                                xqb[:, c * 128:(c + 1) * 128],
                                self.ident[:]),
                            waits=w, inc=(c4 == 3))
                    tev = self.ev(lambda e, g=g, pp=pp: e.tensor_copy(
                        stg[:, g * 512:(g + 1) * 512], pp[:, 0:512]),
                        waits=(tts,))
                    ptr_war[g % 2] = tev
                    tstg = tev
                prev_q_free = self.edma(
                    "gpsimd",
                    self.d_sxqt.rearrange("(c p) n -> p c n", p=128)[
                        :, :, j * 128:(j + 1) * 128],
                    stg[:].rearrange("p (c n) -> p c n", c=16),
                    waits=(tstg,), sem=dstg)

        self.vgap()
        f_v = self.vgap()
        self.agap()
        f_a = self.agap()
        f_t = self.et(lambda e: e.transpose(
            ptrs[0][:, 0:128], self.ident[:], self.ident[:]),
            waits=(ptr_war[0],))
        f_p = self.pgap()
        f_sy = self.P.emit("sync", lambda e: e.nop(),
                           waits=self.dma_fence_waits(), inc=self.sP)
        self.P0F = (f_v, f_a, f_t, f_p, f_sy)
        p0.close()

    def phase1(self):
        P0F = self.P0F
        nc = self.nc
        p1o = _Scope()
        s1o = p1o.stack(nc)
        p1a = _Scope()
        s1a = p1a.stack(nc)

        mslab = [self.sb(f"msl{i}", [128, 1024], F32, s1o) for i in range(2)]
        mjunk = self.sb("mjunk", [128, 1024], BF16, s1o)
        smp1t = self.sb("smp1", [128, 256], F32, s1o)
        _s1 = [0]

        def slot1(n=1):
            a = _s1[0]
            _s1[0] += n
            assert _s1[0] <= 256
            return smp1t[:, a:a + n]

        nmP = [slot1(16), slot1(16)]
        lP = [slot1(16), slot1(16)]
        e_ = slot1(16)
        d_ = slot1(16)
        lw = slot1(16)
        L_ = slot1(4)
        r_ = slot1(4)
        sf = slot1(16)
        nl = slot1(2)
        nmc = slot1(4)
        nmt = slot1(8)
        msc = slot1(2)

        wrot = [self.sb(f"wrot{i}", [128, 16 * 256], BF16, s1a)
                for i in range(2)]
        q1b = self.sb("q1b", [128, 2 * 1024], BF16, s1a)
        q2b = self.sb("q2b", [128, 2 * 1024], BF16, s1a)
        kbf = [self.sb(f"kbf{i}", [128, T], BF16, s1a) for i in range(2)]
        vbf = [self.sb(f"vbf{i}", [128, T], BF16, s1a) for i in range(2)]
        pbuf = [self.sb(f"pbuf{i}", [128, 512], BF16, s1a) for i in range(4)]
        pTb = [self.sb(f"pTb{i}", [128, 512], BF16, s1a) for i in range(4)]
        ysg = self.sb("ysg", [128, 16 * 128], F32, s1a)
        ystg = self.sb("ystg", [128, 256], BF16, s1a)
        ytmp = [self.sb(f"ytmp{i}", [128, 128], F32, s1a) for i in range(2)]
        wlamb = self.sb("wlamb", [128, 16 * 16], BF16, s1a)

        stash = [self.psb(f"stash{i}", [128, 512], F32, s1a)
                 for i in range(4)]
        trp = [self.psb(f"trp{i}", [128, 512], BF16, s1a) for i in range(2)]
        yp = [self.psb(f"yp{i}", [128, 512], F32, s1a) for i in range(2)]
        prp = stash[3]

        ev, ea, et, ep, edma = self.ev, self.ea, self.et, self.ep, self.edma
        vgap, agap = self.vgap, self.agap
        xnT_c, xnTq_c = self.xnT_c, self.xnTq_c
        masks, lam, ident = self.masks, self.lam, self.ident

        mean_srcs = []
        for mat, dsrc in (("1", self.din["w1t"]), ("2", self.din["w2t"])):
            for rc in range(dsrc.tensor.shape[0] // 128):
                for cs in range(dsrc.tensor.shape[1] // 1024):
                    mean_srcs.append(
                        (mat, dsrc[rc * 128:(rc + 1) * 128,
                                   cs * 1024:(cs + 1) * 1024]))
        macc1, macc2 = self.mw1p, self.mw2p
        dmsl = [self.mkdsem("dmsl0"), self.mkdsem("dmsl1")]
        dwr = [self.mkdsem("dwr0"), self.mkdsem("dwr1")]
        dwlam = self.mkdsem("dwlam")
        dwob = [self.mkdsem("dwob0"), self.mkdsem("dwob1")]
        dxpt = [self.mkdsem("dxpt0"), self.mkdsem("dxpt1")]
        dspt = self.mkdsem("dspt")
        mtmp = self.slot(8)
        mst = {"i": 0, "tmp_tok": {}, "done": [], "zeroed": False}

        def emit_mean_step():
            i = mst["i"]
            if i >= len(mean_srcs):
                return
            mst["i"] += 1
            if not mst["zeroed"]:
                ev(lambda e: e.memset(macc1[:, :], 0.0), waits=P0F,
                   inc=False)
                ev(lambda e: e.memset(macc2[:, :], 0.0), inc=False)
                mst["zeroed"] = True
            par = i % 2
            mat, srcap = mean_srcs[i]
            dtok = edma("scalar", mslab[par][:], srcap,
                        waits=(mst["done"][i - 2] if i >= 2 else P0F[1],),
                        sem=dmsl[par])
            ts_ = mtmp[:, i % 8:i % 8 + 1]
            atok = ea(lambda e, par=par, ts_=ts_: e.activation(
                mjunk[:], mslab[par][:], AF.Abs, accum_out=ts_),
                waits=(dtok, mst["tmp_tok"].get(i % 8)))
            mst["done"].append(atok)
            acc = macc1 if mat == "1" else macc2
            k = i % 16
            vtok = ev(lambda e, acc=acc, k=k, ts_=ts_: e.tensor_add(
                acc[:, k:k + 1], acc[:, k:k + 1], ts_), waits=(atok,))
            mst["tmp_tok"][i % 8] = vtok

        tok_wlam = edma("gpsimd", wlamb[:].rearrange(
            "p (c n) -> p c n", c=16),
            self.din["wlam"].rearrange("(c p) n -> p c n", p=128),
            waits=(P0F[4],), sem=dwlam)
        prp_war = None
        for tt in range(8):
            mmt = None
            for ic in range(16):
                w = ([P0F[2], tok_wlam, prp_war, self.CONST]
                     if ic == 0 else [])
                mmt = et(lambda e, ic=ic, tt=tt: e.matmul(
                    prp[:, 0:16],
                    xnTq_c(ic, slice(tt * 128, (tt + 1) * 128)),
                    wlamb[:, ic * 16:(ic + 1) * 16],
                    start=(ic == 0), stop=(ic == 15)), waits=w,
                    inc=(ic == 15))
            prp_war = ea(lambda e, tt=tt: e.activation(
                lam[:, tt * 16:(tt + 1) * 16], prp[:, 0:16], AF.Sigmoid),
                waits=(mmt,))
        lam_gate = agap()

        w_war = [None, None]
        stash_free = [None, None, None, None]
        pb_free = [None, None, None, None]
        pT_tok = [None, None, None, None]
        yp_free = [None, None]
        trp_war = [None, None]
        nm_war = {}
        l_war = {}
        prev_combine = None
        gseg = [0]

        SEGLIST = []
        for t in range(NQ):
            for sig in range(4):
                for s in range(NKB[t]):
                    SEGLIST.append((t, sig, s))

        def emit_QK(t, sig, s, i, gate):
            par = i % 4
            tp = t % 2
            he = sig >> 1
            qb = q1b if (sig & 1) == 0 else q2b
            kv = sig & 1
            idx = s * 4 + sig
            mmt = et(lambda e, par=par, s=s, he=he, t=t, qb=qb,
                     kv=kv: e.matmul(
                stash[par][:, 0:512],
                qb[:, he * 1024 + t * 128:he * 1024 + (t + 1) * 128],
                kbf[kv][:, s * 512:(s + 1) * 512],
                start=True, stop=True),
                waits=(stash_free[par], gate), inc=True)
            vt = mmt
            if s == NKB[t] - 1:
                vt = ev(lambda e, par=par, t=t: e.tensor_tensor(
                    stash[par][:, 0:512], stash[par][:, 0:512],
                    masks[:, t * 512:(t + 1) * 512], ALU.add),
                    waits=(mmt, self.tok_msk))
            ev(lambda e, par=par, tp=tp, idx=idx: e.reduce_max(
                nmP[tp][:, idx:idx + 1], stash[par][:, 0:512], AX.X,
                negate=True), waits=(vt, nm_war.get((tp, sig, s))),
                inc=False)
            vgap()
            vgap()
            rm2 = vgap()
            ext = ea(lambda e, par=par, tp=tp, idx=idx: e.activation(
                pbuf[par][:], stash[par][:, 0:512], AF.Exp,
                bias=nmP[tp][:, idx:idx + 1], scale=1.0,
                accum_out=lP[tp][:, idx:idx + 1]),
                waits=(rm2, pb_free[par], l_war.get((tp, sig, s))))
            nm_war[(tp, sig, s)] = ext
            stash_free[par] = ext
            return ext

        def emit_TRPV(t, sig, s, i, exp_tok):
            par = i % 4
            tpp = i % 2
            kv = sig & 1
            trt = None
            for kc in range(4):
                w = [exp_tok, trp_war[tpp]] if kc == 0 else []
                trt = et(lambda e, kc=kc, par=par, tpp=tpp: e.transpose(
                    trp[tpp][:, kc * 128:(kc + 1) * 128],
                    pbuf[par][:, kc * 128:(kc + 1) * 128], ident[:]),
                    waits=w, inc=(kc == 3))
            tev = ev(lambda e, par=par, tpp=tpp: e.tensor_copy(
                pTb[par][:], trp[tpp][:]), waits=(trt, pT_tok[par]))
            trp_war[tpp] = tev
            pvt = None
            for kc in range(4):
                w = [tev, yp_free[tpp]] if kc == 0 else []
                pvt = et(lambda e, kc=kc, par=par, s=s, kv=kv, tpp=tpp:
                         e.matmul(
                    yp[tpp][:, 0:128],
                    pTb[par][:, kc * 128:(kc + 1) * 128],
                    vbf[kv][:, (4 * s + kc) * 128:(4 * s + kc + 1) * 128],
                    start=(kc == 0), stop=(kc == 3)),
                    waits=w, inc=(kc == 3))
            pb_free[par] = pvt
            pT_tok[par] = pvt
            idx = s * 4 + sig
            yev = ev(lambda e, tpp=tpp, idx=idx: e.tensor_copy(
                ysg[:, idx * 128:(idx + 1) * 128],
                yp[tpp][:, 0:128]), waits=(pvt,))
            yp_free[tpp] = yev
            return yev

        def emit_combine(pj, t, yev_tok):
            nonlocal prev_combine
            ns = NKB[t]
            tp = t % 2
            nm0 = nmP[tp]
            l0 = lP[tp]
            ag = agap()
            if ns == 1:
                ev(lambda e: e.reciprocal(sf[:, 0:4], l0[:, 0:4]),
                   waits=(ag, yev_tok, lam_gate), inc=False)
                vgap()
                vgap()
            else:
                if ns == 2:
                    ev(lambda e: e.tensor_tensor(
                        nmc, nm0[:, 0:4], nm0[:, 4:8], ALU.min),
                        waits=(ag, yev_tok, lam_gate), inc=False)
                elif ns == 3:
                    ev(lambda e: e.tensor_tensor(
                        nmt[:, 0:4], nm0[:, 0:4], nm0[:, 4:8], ALU.min),
                        waits=(ag, yev_tok, lam_gate), inc=False)
                    vgap()
                    ev(lambda e: e.tensor_tensor(
                        nmc, nmt[:, 0:4], nm0[:, 8:12], ALU.min),
                        inc=False)
                else:
                    ev(lambda e: e.tensor_tensor(
                        nmt[:, 0:4], nm0[:, 0:4], nm0[:, 4:8], ALU.min),
                        waits=(ag, yev_tok, lam_gate), inc=False)
                    ev(lambda e: e.tensor_tensor(
                        nmt[:, 4:8], nm0[:, 8:12], nm0[:, 12:16], ALU.min),
                        inc=False)
                    vgap()
                    ev(lambda e: e.tensor_tensor(
                        nmc, nmt[:, 0:4], nmt[:, 4:8], ALU.min), inc=False)
                vgap()
                vgap()
                for s in range(ns):
                    ev(lambda e, s=s: e.tensor_sub(
                        e_[:, s * 4:(s + 1) * 4], nmc,
                        nm0[:, s * 4:(s + 1) * 4]), inc=False)
                vgap()
                t1 = vgap()
                ea(lambda e, ns=ns: e.activation(
                    d_[:, 0:4 * ns], e_[:, 0:4 * ns], AF.Exp),
                    waits=(t1,), inc=False)
                ag2 = agap()
                ev(lambda e, ns=ns: e.tensor_tensor(
                    lw[:, 0:4 * ns], l0[:, 0:4 * ns], d_[:, 0:4 * ns],
                    ALU.mult), waits=(ag2,), inc=False)
                vgap()
                vgap()
                if ns == 2:
                    ev(lambda e: e.tensor_add(L_, lw[:, 0:4], lw[:, 4:8]),
                       inc=False)
                elif ns == 3:
                    ev(lambda e: e.tensor_add(nmt[:, 0:4], lw[:, 0:4],
                                              lw[:, 4:8]), inc=False)
                    vgap()
                    vgap()
                    ev(lambda e: e.tensor_add(L_, nmt[:, 0:4],
                                              lw[:, 8:12]), inc=False)
                else:
                    ev(lambda e: e.tensor_add(nmt[:, 0:4], lw[:, 0:4],
                                              lw[:, 4:8]), inc=False)
                    ev(lambda e: e.tensor_add(nmt[:, 4:8], lw[:, 8:12],
                                              lw[:, 12:16]), inc=False)
                    vgap()
                    vgap()
                    ev(lambda e: e.tensor_add(L_, nmt[:, 0:4],
                                              nmt[:, 4:8]), inc=False)
                vgap()
                vgap()
                ev(lambda e: e.reciprocal(r_, L_), inc=False)
                vgap()
                vgap()
                for s in range(ns):
                    ev(lambda e, s=s: e.tensor_tensor(
                        sf[:, s * 4:(s + 1) * 4], d_[:, s * 4:(s + 1) * 4],
                        r_, ALU.mult), inc=False)
                vgap()
                vgap()
            for he in range(2):
                hcol = t * 16 + 2 * pj + he
                ev(lambda e, he=he, hcol=hcol: e.tensor_scalar_mul(
                    nl[:, he:he + 1], lam[:, hcol:hcol + 1], -1.0),
                    inc=False)
            vgap()
            vgap()
            for s in range(ns):
                for he in range(2):
                    c = s * 4 + 1 + 2 * he
                    ev(lambda e, c=c, he=he: e.tensor_tensor(
                        sf[:, c:c + 1], sf[:, c:c + 1],
                        nl[:, he:he + 1], ALU.mult), inc=False)
            vgap()
            sfg = vgap()
            toks = []
            for he in range(2):
                tmp = ytmp[he]
                terms = [(s * 4 + 2 * he + j, s)
                         for s in range(ns) for j in range(2)]
                first = True
                tk = None
                for n_, (idx, s) in enumerate(terms):
                    ysl = ysg[:, idx * 128:(idx + 1) * 128]
                    sfc = sf[:, idx:idx + 1]
                    last = (n_ == len(terms) - 1)
                    dst = (ystg[:, he * 128:(he + 1) * 128] if last
                           else tmp[:])
                    if first:
                        tk = ev(lambda e, ysl=ysl, sfc=sfc, dst=dst:
                                e.tensor_scalar_mul(dst, ysl, sfc),
                                waits=(sfg,), inc=last)
                        first = False
                    else:
                        tk = ev(lambda e, ysl=ysl, sfc=sfc, dst=dst,
                                tmp=tmp: e.scalar_tensor_tensor(
                            dst, ysl, sfc, tmp[:], ALU.mult, ALU.add),
                            inc=last)
                toks.append(tk)
            trt = None
            for he in range(2):
                trt = et(lambda e, he=he: e.transpose(
                    trp[0][:, he * 128:(he + 1) * 128],
                    ystg[:, he * 128:(he + 1) * 128], ident[:]),
                    waits=(toks[1], trp_war[0]) if he == 0 else (),
                    inc=(he == 1))
            dst = self.yT[:].rearrange("p (c n) -> p c n", c=16)[
                :, 2 * pj:2 * pj + 2, t * 128:(t + 1) * 128]
            tev = ev(lambda e, dst=dst: e.tensor_copy(
                dst, trp[0][:, 0:256].rearrange("p (c n) -> p c n", c=2)),
                waits=(trt,))
            trp_war[0] = tev
            prev_combine = tev
            for s in range(ns):
                for sig in range(4):
                    l_war[(tp, sig, s)] = tev
            return tev

        for pj in range(8):
            c0 = 2 * pj * 128
            tq1 = edma("gpsimd", wrot[0][:].rearrange(
                "p (c n) -> p c n", c=16),
                self.din["wq1"].rearrange("(c p) n -> p c n", p=128)[
                    :, :, c0:c0 + 256], waits=(w_war[0], P0F[4]),
                sem=dwr[0])
            tq2 = edma("gpsimd", wrot[1][:].rearrange(
                "p (c n) -> p c n", c=16),
                self.din["wq2"].rearrange("(c p) n -> p c n", p=128)[
                    :, :, c0:c0 + 256], waits=(w_war[1], P0F[4]),
                sem=dwr[1])
            lastq1 = None
            for he in range(2):
                for th in range(2):
                    mmt = None
                    for ic in range(16):
                        w = ([tq1, prp_war, P0F[2], stash_free[3]]
                             if ic == 0 else [])
                        mmt = et(lambda e, ic=ic, he=he, th=th: e.matmul(
                            prp[:, 0:512],
                            wrot[0][:, ic * 256 + he * 128:
                                     ic * 256 + (he + 1) * 128],
                            xnTq_c(ic, slice(th * 512, (th + 1) * 512)),
                            start=(ic == 0), stop=(ic == 15)),
                            waits=w, inc=(ic == 15))
                    lastq1 = mmt
                    prp_war = ea(lambda e, he=he, th=th: e.activation(
                        q1b[:, he * 1024 + th * 512:he * 1024 +
                            (th + 1) * 512], prp[:, 0:512], AF.Identity,
                        scale=ISQD), waits=(mmt,))
            w_war[0] = lastq1
            tkA = edma("gpsimd", wrot[0][:].rearrange(
                "p (c n) -> p c n", c=16)[:, :, 0:128],
                self.din["wk"].rearrange("(c p) n -> p c n", p=128)[
                    :, :, pj * 128:(pj + 1) * 128],
                waits=(w_war[0],), sem=dwr[0])
            tkB = edma("gpsimd", wrot[0][:].rearrange(
                "p (c n) -> p c n", c=16)[:, :, 128:256],
                self.din["wk"].rearrange("(c p) n -> p c n", p=128)[
                    :, :, (8 + pj) * 128:(9 + pj) * 128], sem=dwr[0])
            lastq2 = None
            for he in range(2):
                for th in range(2):
                    mmt = None
                    for ic in range(16):
                        w = [tq2, prp_war] if ic == 0 else []
                        mmt = et(lambda e, ic=ic, he=he, th=th: e.matmul(
                            prp[:, 0:512],
                            wrot[1][:, ic * 256 + he * 128:
                                     ic * 256 + (he + 1) * 128],
                            xnTq_c(ic, slice(th * 512, (th + 1) * 512)),
                            start=(ic == 0), stop=(ic == 15)),
                            waits=w, inc=(ic == 15))
                    lastq2 = mmt
                    prp_war = ea(lambda e, he=he, th=th: e.activation(
                        q2b[:, he * 1024 + th * 512:he * 1024 +
                            (th + 1) * 512], prp[:, 0:512], AF.Identity,
                        scale=ISQD), waits=(mmt,))
            w_war[1] = lastq2
            tvA = edma("gpsimd", wrot[1][:].rearrange(
                "p (c n) -> p c n", c=16)[:, :, 0:128],
                self.din["wv"].rearrange("(c p) n -> p c n", p=128)[
                    :, :, pj * 128:(pj + 1) * 128],
                waits=(w_war[1],), sem=dwr[1])
            tvB = edma("gpsimd", wrot[1][:].rearrange(
                "p (c n) -> p c n", c=16)[:, :, 128:256],
                self.din["wv"].rearrange("(c p) n -> p c n", p=128)[
                    :, :, (8 + pj) * 128:(9 + pj) * 128], sem=dwr[1])
            lastk = None
            for hx in range(2):
                for tq in range(4):
                    mmt = None
                    for ic in range(16):
                        w = [tkA, tkB, prp_war] if ic == 0 else []
                        mmt = et(lambda e, ic=ic, hx=hx, tq=tq: e.matmul(
                            prp[:, 0:512],
                            wrot[0][:, ic * 256 + hx * 128:
                                     ic * 256 + (hx + 1) * 128],
                            xnT_c(ic, slice(tq * 512, (tq + 1) * 512)),
                            start=(ic == 0), stop=(ic == 15)),
                            waits=w, inc=(ic == 15))
                    lastk = mmt
                    prp_war = ev(lambda e, hx=hx, tq=tq: e.tensor_copy(
                        kbf[hx][:, tq * 512:(tq + 1) * 512], prp[:, 0:512]),
                        waits=(mmt,))
            w_war[0] = lastk
            lastv = None
            for tv in range(16):
                mmt = None
                for ic in range(16):
                    w = [tvA, tvB, prp_war] if ic == 0 else []
                    mmt = et(lambda e, ic=ic, tv=tv: e.matmul(
                        prp[:, 0:256],
                        xnT_c(ic, slice(tv * 128, (tv + 1) * 128)),
                        wrot[1][:, ic * 256:(ic + 1) * 256],
                        start=(ic == 0), stop=(ic == 15)),
                        waits=w, inc=(ic == 15))
                lastv = mmt
                ev(lambda e, tv=tv: e.tensor_copy(
                    vbf[0][:, tv * 128:(tv + 1) * 128], prp[:, 0:128]),
                    waits=(mmt,), inc=False)
                prp_war = ev(lambda e, tv=tv: e.tensor_copy(
                    vbf[1][:, tv * 128:(tv + 1) * 128], prp[:, 128:256]))
            w_war[1] = lastv
            pair_gate = prp_war

            for _ in range(4):
                emit_mean_step()

            from collections import deque
            pending = deque()
            for nseg, (t, sig, s) in enumerate(SEGLIST):
                i = gseg[0]
                ext = emit_QK(t, sig, s, i,
                              gate=pair_gate if nseg == 0 else None)
                is_end = (sig == 3 and s == NKB[t] - 1)
                pending.append((t, sig, s, i, ext, is_end))
                gseg[0] += 1
                if len(pending) >= 4:
                    (pt, psig, ps, pi, pext, pend) = pending.popleft()
                    yev = emit_TRPV(pt, psig, ps, pi, pext)
                    if pend:
                        emit_combine(pj, pt, yev)
                        for _ in range(3):
                            emit_mean_step()
            while pending:
                (pt, psig, ps, pi, pext, pend) = pending.popleft()
                yev = emit_TRPV(pt, psig, ps, pi, pext)
                if pend:
                    emit_combine(pj, pt, yev)

        while mst["i"] < len(mean_srcs):
            emit_mean_step()

        p1a.close()
        p1w = _Scope()
        s1w = p1w.stack(nc)
        wob = [self.sb(f"wob{i}", [128, 16 * 512], BF16, s1w)
               for i in range(2)]
        xpt = [self.sb(f"xpt{i}", [128, 512], F32, s1w) for i in range(2)]
        pot = [self.sb(f"pot{i}", [128, 512], F32, s1w) for i in range(2)]
        wop = self.psb("wop", [128, 512], F32, s1w)
        att_v = self.vgap()
        att_a = self.agap()
        att_t = self.P.emit("tensor", lambda e: e.matmul(
            wop[:, 0:16], self.xnTq[:, 0:128], self.xnTq[:, 0:16],
            start=True, stop=True), waits=(), inc=self.sT)
        ATT = (prev_combine, att_v, att_a, att_t)
        wob_war = [ATT, ATT]
        pot_war = [ATT, ATT]
        xpt_war = [ATT, ATT]
        wop_war = att_t
        last_part = None
        for db in range(4):
            ww = wob_war[db % 2]
            ww = ww if isinstance(ww, tuple) and len(ww) and isinstance(
                ww[0], tuple) else (ww,)
            dwo = edma("gpsimd", wob[db % 2][:].rearrange(
                "p (c n) -> p c n", c=16),
                self.din["wo"].rearrange("(c p) n -> p c n", p=128)[
                    :, :, db * 512:(db + 1) * 512],
                waits=ww, sem=dwob[db % 2])
            for tt in range(8):
                i = db * 8 + tt
                mmt = None
                for ic in range(16):
                    w = [dwo, wop_war, prev_combine] if ic == 0 else []
                    mmt = et(lambda e, ic=ic, tt=tt, db=db: e.matmul(
                        wop[:, 0:512],
                        self.yT[:, ic * 1024 + tt * 128:
                                ic * 1024 + (tt + 1) * 128],
                        wob[db % 2][:, ic * 512:(ic + 1) * 512],
                        start=(ic == 0), stop=(ic == 15)),
                        waits=w, inc=(ic == 15))
                wob_war[db % 2] = mmt
                xw = xpt_war[i % 2]
                xw = xw if isinstance(xw, tuple) and len(xw) and isinstance(
                    xw[0], tuple) else (xw,)
                dxp = edma("sync", xpt[i % 2][:],
                           self.din["xq"][tt * 128:(tt + 1) * 128,
                                          db * 512:(db + 1) * 512],
                           waits=xw, sem=dxpt[i % 2])
                pw = pot_war[i % 2]
                pw = pw if isinstance(pw, tuple) and len(pw) and isinstance(
                    pw[0], tuple) else (pw,)
                ptok = ev(lambda e, i=i: e.tensor_add(
                    pot[i % 2][:], wop[:, 0:512], xpt[i % 2][:]),
                    waits=(mmt, dxp) + pw)
                wop_war = ptok
                xpt_war[i % 2] = ptok
                last_part = edma("sync", self.d_spart[
                    tt * 128:(tt + 1) * 128, db * 512:(db + 1) * 512],
                    pot[i % 2][:], waits=(ptok,), sem=dspt)
                pot_war[i % 2] = last_part

        fin = []
        for acc, sw, mn in ((macc1, self.sw1, self.mean1),
                            (macc2, self.sw2, self.mean2)):
            ev(lambda e, acc=acc: e.reduce_sum(
                msc[:, 0:1], acc[:, :], AX.X),
                waits=(mst["done"][-1],), inc=False)
            vgap()
            g1 = vgap()
            mm1 = et(lambda e: e.matmul(
                wop[0:1, 0:1], self.onesc[:], msc[:, 0:1],
                start=True, stop=True), waits=(g1, wop_war))
            cv = ev(lambda e: e.tensor_scalar(
                msc[0:1, 1:2], wop[0:1, 0:1],
                1.0 / (D * INNER), QEPS, ALU.mult, ALU.max), waits=(mm1,))
            vgap()
            g2 = vgap()
            mm2 = et(lambda e: e.matmul(
                wop[:, 4:5], self.onesr[:], msc[0:1, 1:2],
                start=True, stop=True), waits=(g2,))
            cb = ev(lambda e, mn=mn: e.tensor_copy(mn, wop[:, 4:5]),
                    waits=(mm2,))
            vgap()
            rc_ = ev(lambda e, sw=sw, mn=mn: e.reciprocal(sw, mn),
                     waits=(cb,))
            vgap()
            g3 = vgap()
            wop_war = rc_
            fin.append(g3)

        self.vgap(waits=(last_part,))
        f_v = self.vgap()
        self.agap()
        f_a = self.agap()
        f_t = et(lambda e: e.transpose(trp[0][:, 0:128], ident[:],
                                        ident[:]), waits=tuple(trp_war))
        f_p = self.pgap()
        f_sy = self.P.emit("sync", lambda e: e.nop(),
                           waits=self.dma_fence_waits(), inc=self.sP)
        self.P1F = (f_v, f_a, f_t, f_p, f_sy, fin[0], fin[1])
        p1w.close()
        p1o.close()

    def phase2(self):
        nc = self.nc
        P1F = self.P1F
        ev, ea, et, ep, edma = self.ev, self.ea, self.et, self.ep, self.edma
        vgap, agap, pgap = self.vgap, self.agap, self.pgap
        p2 = _Scope()
        s2 = p2.stack(nc)
        h = self.sb("hbuf", [128, 8 * INNER], F16, s2)

        nsw1 = self.slot()
        nsw2 = self.slot()
        bm05 = self.slot()
        bp05 = self.slot()
        bbig = self.slot()
        ev(lambda e: e.memset(bm05, -0.5), inc=False)
        ev(lambda e: e.memset(bp05, 0.5), inc=False)
        ev(lambda e: e.memset(bbig, BIGC + 0.5), inc=False)
        hq_ssq = self.slot(8)
        hq_ma = self.slot(8)
        hq_n = self.slot(8)
        hq_rn = self.slot(8)
        hq_macl = self.slot(8)
        hq_tsc = self.slot(8)
        ev(lambda e: e.tensor_scalar(
            self.g1v, self.fq_sxo, self.mean1, 1.0 / 127.0, ALU.mult,
            ALU.mult), waits=P1F, inc=False)
        ev(lambda e: e.tensor_scalar_mul(nsw1, self.sw1, -1.0), inc=False)
        ev(lambda e: e.tensor_scalar_mul(nsw2, self.sw2, -1.0), inc=False)
        vgap()
        scgate = vgap()

        p2a = _Scope()
        s2a = p2a.stack(nc)
        xqTs = self.sb("xqTs", [128, 16 * 1024], BF16, s2a)
        NW1 = 4
        w1rot = [self.sb(f"w1r{i}", [128, 512], F32, s2a)
                 for i in range(NW1)]
        wqs = [self.sb(f"wqs{i}", [128, 16 * 512], BF16, s2a)
               for i in range(2)]
        gtmp = self.sb("gtmp", [128, 512], F32, s2a)
        pq = [self.sb(f"pq{i}", [128, 512], BF16, s2a) for i in range(4)]
        ps1 = [self.psb(f"ps1_{i}", [128, 512], F32, s2a) for i in range(8)]

        dxq2 = self.mkdsem("dxq2")
        dw1 = [self.mkdsem(f"dw1{i}") for i in range(4)]
        dw2 = [self.mkdsem(f"dw2{i}") for i in range(4)]
        dxp2 = [self.mkdsem("dxp2a"), self.mkdsem("dxp2b")]
        dout = self.mkdsem("dout")
        txq = edma("sync", xqTs[:].rearrange("p (c n) -> p c n", c=16),
                   self.d_sxqt.rearrange("(c p) n -> p c n", p=128),
                   waits=P1F, sem=dxq2)
        w1r_war = [scgate] * NW1
        pq_war = [scgate, scgate]
        wq_war = [None, None]
        ps1_free = [None] * 8
        gt_war = None
        for ob in range(16):
            wq_tok = None
            for ic in range(16):
                dtok = edma("sync", w1rot[ic % NW1][:],
                            self.din["w1t"][ic * 128:(ic + 1) * 128,
                                            ob * 512:(ob + 1) * 512],
                            waits=(w1r_war[ic % NW1],), sem=dw1[ic % NW1])
                w = [dtok, scgate, pq_war[ic % 2]]
                if ic == 0:
                    w.append(wq_war[ob % 2])
                pp0, pp1 = pq[2 * (ic % 2)], pq[2 * (ic % 2) + 1]
                ev(lambda e, pp0=pp0, ic=ic: e.tensor_scalar(
                    pp0[:], w1rot[ic % NW1][:], self.sw1, 0.5, ALU.mult,
                    ALU.is_gt), waits=w, inc=False)
                cpn = ev(lambda e, pp1=pp1, ic=ic: e.tensor_scalar(
                    pp1[:], w1rot[ic % NW1][:], nsw1, 0.5, ALU.mult,
                    ALU.is_gt))
                w1r_war[ic % NW1] = cpn
                ev(lambda e, pp0=pp0, pp1=pp1, ic=ic, ob=ob:
                   e.tensor_sub(
                    wqs[ob % 2][:, ic * 512:(ic + 1) * 512], pp0[:],
                    pp1[:]), inc=False)
                wq_tok = vgap()
                pq_war[ic % 2] = wq_tok
            mm_tok = [None] * 8
            for tt in range(8):
                mmt = None
                for ic in range(16):
                    w = [wq_tok, ps1_free[tt], txq] if ic == 0 else []
                    mmt = et(lambda e, ic=ic, tt=tt, ob=ob: e.matmul(
                        ps1[tt][:, 0:512],
                        xqTs[:, ic * 1024 + tt * 128:
                             ic * 1024 + (tt + 1) * 128],
                        wqs[ob % 2][:, ic * 512:(ic + 1) * 512],
                        start=(ic == 0), stop=(ic == 15)),
                        waits=w, inc=(ic == 15))
                mm_tok[tt] = mmt
            wq_war[ob % 2] = mm_tok[7]
            for tt in range(8):
                a1 = ea(lambda e, tt=tt: e.activation(
                    gtmp[:], ps1[tt][:, 0:512], AF.Gelu,
                    scale=self.g1v[:, tt:tt + 1]),
                    waits=(mm_tok[tt], gt_war))
                ps1_free[tt] = a1
                a2 = ea(lambda e, tt=tt, ob=ob: e.activation(
                    h[:, tt * INNER + ob * 512:tt * INNER + (ob + 1) * 512],
                    gtmp[:], AF.Gelu))
                gt_war = a2
        ffn1_act = gt_war
        p2a.close()
        if self.upto == 3:
            p2.close()
            self._stub_out(ffn1_act)
            return

        p2b = _Scope()
        s2b = p2b.stack(nc)
        hqs = self.sb("hqs", [128, INNER], F16, s2b)
        idf16 = self.sb("idf16", [128, 128], F16, s2b)
        ev(lambda e: e.tensor_copy(idf16[:], self.ident[:]),
           waits=(ffn1_act,), inc=False)
        vgap()
        idf_tok = vgap()
        qu = self.sb("qu2", [128, 2048], F32, s2b)
        qui = self.sb("qui2", [128, 2048], I32, s2b)
        qc1 = self.sb("qc12", [128, 2048], F32, s2b)
        trp2 = self.psb("trp2", [128, 1024], F16, s2b)

        def hsl(tt, a, b):
            return h[:, tt * INNER + a:tt * INNER + b]

        hq_done = []
        trp2_war = None
        hqs_war = None
        for tt in range(8):
            sq = ea(lambda e, tt=tt: e.activation(
                hqs[:], hsl(tt, 0, INNER), AF.Square,
                accum_out=hq_ssq[:, tt:tt + 1]),
                waits=(ffn1_act, hqs_war))
            mx = ev(lambda e, tt=tt: e.reduce_max(
                hq_ma[:, tt:tt + 1], hsl(tt, 0, INNER), AX.X,
                apply_absolute_value=True), waits=(sq,))
            agap()
            sqa = agap()
            ea(lambda e, tt=tt: e.activation(
                hq_n[:, tt:tt + 1], hq_ssq[:, tt:tt + 1], AF.Sqrt),
                waits=(sqa,), inc=False)
            na = agap()
            ev(lambda e, tt=tt: e.tensor_scalar_max(
                hq_n[:, tt:tt + 1], hq_n[:, tt:tt + 1], 1e-12),
                waits=(na, mx), inc=False)
            vgap()
            ev(lambda e, tt=tt: e.reciprocal(
                hq_rn[:, tt:tt + 1], hq_n[:, tt:tt + 1]), inc=False)
            vgap()
            ev(lambda e, tt=tt: e.tensor_tensor(
                hq_macl[:, tt:tt + 1], hq_ma[:, tt:tt + 1],
                hq_rn[:, tt:tt + 1], ALU.mult), inc=False)
            vgap()
            ev(lambda e, tt=tt: e.tensor_scalar(
                hq_macl[:, tt:tt + 1], hq_macl[:, tt:tt + 1],
                math.sqrt(INNER), QEPS, ALU.mult, ALU.max), inc=False)
            vgap()
            ev(lambda e, tt=tt: e.tensor_scalar(
                self.hsv[:, tt:tt + 1], hq_macl[:, tt:tt + 1],
                self.mean2, 1.0 / 127.0, ALU.mult, ALU.mult), inc=False)
            vgap()
            ev(lambda e, tt=tt: e.reciprocal(
                hq_tsc[:, tt:tt + 1], hq_macl[:, tt:tt + 1]), inc=False)
            vgap()
            ev(lambda e, tt=tt: e.tensor_tensor(
                hq_tsc[:, tt:tt + 1], hq_tsc[:, tt:tt + 1],
                hq_rn[:, tt:tt + 1], ALU.mult), inc=False)
            vgap()
            ev(lambda e, tt=tt: e.tensor_scalar_mul(
                hq_tsc[:, tt:tt + 1], hq_tsc[:, tt:tt + 1],
                127.0 * math.sqrt(INNER)), inc=False)
            vgap()
            tscg = vgap()
            lastq = None
            for qq in range(4):
                a0, b0 = qq * 2048, (qq + 1) * 2048
                pu = ea(lambda e, tt=tt, a0=a0, b0=b0: e.activation(
                    qu[:], hsl(tt, a0, b0), AF.Identity,
                    bias=bbig, scale=hq_tsc[:, tt:tt + 1]),
                    waits=(tscg, lastq))
                ev(lambda e: e.tensor_copy(qui[:], qu[:]), waits=(pu,),
                   inc=False)
                ev(lambda e: e.tensor_copy(qc1[:], qui[:]), inc=False)
                ev(lambda e: e.tensor_tensor(qu[:], qc1[:], qu[:],
                                             ALU.is_gt), inc=False)
                lastq = ev(lambda e, a0=a0, b0=b0: e.scalar_tensor_tensor(
                    hqs[:, a0:b0], qc1[:], -BIGC, qu[:], ALU.add,
                    ALU.subtract))
            hw = None
            for g in range(8):
                trt = None
                for k in range(8):
                    oc = g * 8 + k
                    w = [lastq, trp2_war] if k == 0 else []
                    trt = et(lambda e, oc=oc, k=k: e.transpose(
                        trp2[:, k * 128:(k + 1) * 128],
                        hqs[:, oc * 128:(oc + 1) * 128], idf16[:]),
                        waits=w + [idf_tok] if k == 0 else [], inc=(k == 7))
                hw = ev(lambda e, tt=tt, g=g: e.tensor_copy(
                    hsl(tt, g * 1024, (g + 1) * 1024), trp2[:, 0:1024]),
                    waits=(trt,))
                trp2_war = hw
            hqs_war = hw
            hq_done.append(hw)
        p2b.close()
        if self.upto == 4:
            p2.close()
            self._stub_out(hq_done[-1])
            return

        p2c = _Scope()
        s2c = p2c.stack(nc)
        NW2 = 4
        w2rot = [self.sb(f"w2r{i}", [128, 512], F32, s2c)
                 for i in range(NW2)]
        w2q = [self.sb(f"w2q{i}", [128, 512], F16, s2c)
               for i in range(NW2)]
        pq2 = [self.sb(f"pq2_{i}", [128, 512], BF16, s2c) for i in range(4)]
        xpt2 = [self.sb(f"xpt2_{i}", [128, 512], F32, s2c) for i in range(2)]
        pot2 = [self.sb(f"pot2_{i}", [128, 512], F32, s2c) for i in range(2)]
        ps2 = [self.psb(f"ps2_{i}", [128, 512], F32, s2c) for i in range(8)]

        NDB = 1 if self.upto == 5 else (2 if self.upto == 6 else 4)
        hq_gate = hq_done[-1]
        w2r_war = [hq_gate] * NW2
        pq2_war = [hq_gate, hq_gate]
        w2q_war = [hq_gate] * NW2
        ps2_free = [hq_gate] * 8
        xpt2_war = [hq_gate, hq_gate]
        pot2_war = [hq_gate, hq_gate]
        for db in range(NDB):
            mm_last = [None] * 8
            for oc in range(64):
                dtok = edma("sync", w2rot[oc % NW2][:],
                            self.din["w2t"][oc * 128:(oc + 1) * 128,
                                            db * 512:(db + 1) * 512],
                            waits=(w2r_war[oc % NW2],), sem=dw2[oc % NW2])
                w = [dtok, scgate, w2q_war[oc % NW2], pq2_war[oc % 2]]
                qp0, qp1 = pq2[2 * (oc % 2)], pq2[2 * (oc % 2) + 1]
                ev(lambda e, qp0=qp0, oc=oc: e.tensor_scalar(
                    qp0[:], w2rot[oc % NW2][:], self.sw2, 0.5, ALU.mult,
                    ALU.is_gt), waits=w, inc=False)
                cpn = ev(lambda e, qp1=qp1, oc=oc: e.tensor_scalar(
                    qp1[:], w2rot[oc % NW2][:], nsw2, 0.5, ALU.mult,
                    ALU.is_gt))
                w2r_war[oc % NW2] = cpn
                ev(lambda e, qp0=qp0, qp1=qp1, oc=oc:
                   e.tensor_sub(
                    w2q[oc % NW2][:], qp0[:], qp1[:]), inc=False)
                wq2_tok = vgap()
                pq2_war[oc % 2] = wq2_tok
                mmt = None
                for tt in range(8):
                    w = [wq2_tok, hq_gate]
                    if oc == 0:
                        w.append(ps2_free[tt])
                    mmt = et(lambda e, oc=oc, tt=tt, db=db: e.matmul(
                        ps2[tt][:, 0:512],
                        h[:, tt * INNER + oc * 128:
                          tt * INNER + (oc + 1) * 128],
                        w2q[oc % NW2][:], start=(oc == 0), stop=(oc == 63)),
                        waits=w if tt == 0 else ([ps2_free[tt]]
                                                 if oc == 0 else []),
                        inc=(oc == 63 or tt == 7))
                    mm_last[tt] = mmt
                w2q_war[oc % NW2] = mmt
            for tt in range(8):
                i = db * 8 + tt
                dxp = edma("sync", xpt2[i % 2][:],
                           self.d_spart[tt * 128:(tt + 1) * 128,
                                        db * 512:(db + 1) * 512],
                           waits=(xpt2_war[i % 2],), sem=dxp2[i % 2])
                ptk = ev(lambda e, i=i, tt=tt: e.scalar_tensor_tensor(
                    pot2[i % 2][:], ps2[tt][:, 0:512],
                    self.hsv[:, tt:tt + 1], xpt2[i % 2][:],
                    ALU.mult, ALU.add),
                    waits=(mm_last[tt], dxp, pot2_war[i % 2]))
                ps2_free[tt] = ptk
                xpt2_war[i % 2] = ptk
                otok = edma("sync", self.d_out[
                    tt * 128:(tt + 1) * 128, db * 512:(db + 1) * 512],
                    pot2[i % 2][:], waits=(ptk,), sem=dout)
                pot2_war[i % 2] = otok
        if self.upto in (5, 6):
            zb = self.sb("zst3", [128, D - 512 * NDB], F32, s2c)
            tz = self.ev(lambda e: e.memset(zb[:], 0.0),
                         waits=(pot2_war[0], pot2_war[1]))
            for tt in range(8):
                self.edma("sync",
                          self.d_out[tt * 128:(tt + 1) * 128, 512 * NDB:],
                          zb[:], waits=(tz,))
        self.P.emit("sync", lambda e: e.nop(),
                    waits=self.dma_fence_waits(), inc=self.sP)
        p2c.close()
        p2.close()

    def _stub_out(self, gate):
        zb = self.sb("zstub2", [128, D], F32)
        fv = self.vgap(waits=(gate,))
        fa = self.agap()
        fp = self.pgap()
        tz = self.ev(lambda e: e.memset(zb[:], 0.0),
                     waits=(fv, fa, fp))
        for tt in range(8):
            self.edma("sync", self.d_out[tt * 128:(tt + 1) * 128, :],
                      zb[:], waits=(tz,))
        self.P.emit("sync", lambda e: e.nop(),
                    waits=self.dma_fence_waits(), inc=self.sP)

    def emit_taps(self):
        tapdefs = {
            "xnT": (self.xnT[:], [128, 16 * T], BF16),
            "xnTq": (self.xnTq[:], [128, 16 * 1024], BF16),
            "fq_sxo": (self.fq_sxo, [128, 8], F32),
            "yT": (self.yT[:], [128, 16 * 1024], BF16),
            "lam": (self.lam[:], [128, NQ * H], F32),
            "sxo2": (self.smp[:, 0:64], [128, 64], F32),
        }
        for name in self.taps:
            ap, shape, dt_ = tapdefs[name]
            dram = self.mktap(name, shape, dt_)
            self.edma("sync", dram, ap, waits=self.P1F)
        self.P.emit("sync", lambda e: e.nop(),
                    waits=((self.dSY, self.dSY.n),), inc=self.sP)


class _Scope:

    def __init__(self):
        self._st = None

    def stack(self, nc):
        if self._st is None:
            from contextlib import ExitStack
            self._st = ExitStack()
        return self._st

    def close(self):
        if self._st is not None:
            self._st.close()
            self._st = None

    def __enter__(self):
        return self

    def __exit__(self, *a):
        self.close()
        return False


def build_nc(taps=(), upto=2):
    from contextlib import ExitStack
    nc = bass.Bass("TRN2", target_bir_lowering=False,
                   detect_race_conditions=False)
    with ExitStack() as st:
        b = Builder(nc, st, taps, upto)
        b.build()
    return nc, sorted(b.d_taps)


def make_masks(r):
    m = np.zeros((NQ, 128, 512), dtype=np.float32)
    jj = np.arange(512)[None, :]
    pp = np.arange(128)[:, None]
    for t in range(NQ):
        g = 2 * t + r
        delta = g * 128 - (NKB[t] - 1) * 512
        m[t] = np.where(jj <= delta + pp, 0.0, -1e9)
    return m.reshape(NQ * 128, 512).astype(bf16)


def shard_inputs(inputs):
    x = np.asarray(inputs["L0_attn_x"])
    wq1 = np.ascontiguousarray(np.asarray(inputs["L0_attn_Wq1"]))
    wq2 = np.ascontiguousarray(np.asarray(inputs["L0_attn_Wq2"]))
    wk = np.ascontiguousarray(np.asarray(inputs["L0_attn_Wk"]))
    wv = np.ascontiguousarray(np.asarray(inputs["L0_attn_Wv"]))
    wlam = np.ascontiguousarray(np.asarray(inputs["L0_attn_Wlam"]))
    wo = np.ascontiguousarray(np.asarray(inputs["L0_attn_Wo"]))
    w1t = np.ascontiguousarray(np.asarray(inputs["L0_ffn_w1"]).T)
    w2t = np.ascontiguousarray(np.asarray(inputs["L0_ffn_w2"]).T)
    ident = np.eye(128, dtype=bf16)
    in_maps = []
    for c in range(8):
        b, r = c >> 1, c & 1
        rows = np.concatenate([np.arange((2 * t + r) * 128,
                                         (2 * t + r + 1) * 128)
                               for t in range(NQ)])
        in_maps.append({
            "xf": np.ascontiguousarray(x[b]),
            "xq": np.ascontiguousarray(x[b][rows]),
            "wq1": wq1, "wq2": wq2, "wk": wk, "wv": wv, "wo": wo,
            "wlam": wlam, "w1t": w1t, "w2t": w2t,
            "ident": ident, "cmask": make_masks(r),
        })
    return in_maps


def kernel(**inputs):
    nc, _ = build_nc()
    in_maps = shard_inputs(inputs)
    res = run_bass_kernel_spmd(nc, in_maps, core_ids=list(range(8)))
    x = np.asarray(inputs["L0_attn_x"])
    out = np.empty((B, T, D), dtype=np.float32)
    for c in range(8):
        b, r = c >> 1, c & 1
        o = np.asarray(res.results[c]["out"])
        for t in range(NQ):
            out[b, (2 * t + r) * 128:(2 * t + r + 1) * 128, :] = \
                o[t * 128:(t + 1) * 128, :]
    return out


# version 23 (final)
# speedup vs baseline: 1.0965x; 1.0632x over previous
import math
import numpy as np
import ml_dtypes
import concourse.bass as bass
import concourse.mybir as mybir
from concourse.bass_utils import run_bass_kernel_spmd

F32 = mybir.dt.float32
F16 = mybir.dt.float16
BF16 = mybir.dt.bfloat16
I32 = mybir.dt.int32
AF = mybir.ActivationFunctionType
ALU = mybir.AluOpType
AX = mybir.AxisListType
bf16 = ml_dtypes.bfloat16

B, T, D, H, DH, INNER = 4, 2048, 2048, 16, 128, 8192
NQ = 8
NKB = [1, 1, 2, 2, 3, 3, 4, 4]
SEGS = [1, 1, 1, 1, 2, 2, 2, 2]
ISQD = 1.0 / math.sqrt(DH)
BIGC = 384.0
LNEPS = 1e-5
QEPS = 1e-5


class Sem:
    def __init__(self, h, name):
        self.h = h
        self.n = 0
        self.name = name

    def inc(self, k=1):
        self.n += k
        return self.n


class Prog:
    ENGS = ("sync", "gpsimd", "vector", "scalar", "tensor")

    def __init__(self):
        self.ops = {e: [] for e in self.ENGS}

    def emit(self, eng, fn, waits=(), inc=None, inc_k=1):
        tok = None
        incs = []
        if inc is not None:
            v = inc.inc(inc_k)
            incs = [(inc, inc_k)]
            tok = (inc, v)
        self.ops[eng].append((tuple(w for w in waits if w is not None),
                              fn, incs))
        return tok

    def check_deadlock(self):
        pcs = {e: 0 for e in self.ENGS}
        cnt = {}
        total = sum(len(v) for v in self.ops.values())
        done = 0
        progress = True
        while progress:
            progress = False
            for e in self.ENGS:
                while pcs[e] < len(self.ops[e]):
                    waits, _, incs = self.ops[e][pcs[e]]
                    if all(cnt.get(id(s), 0) >= v for s, v in waits):
                        for s, k in incs:
                            cnt[id(s)] = cnt.get(id(s), 0) + k
                        pcs[e] += 1
                        done += 1
                        progress = True
                    else:
                        break
        if done != total:
            detail = []
            for e in self.ENGS:
                if pcs[e] < len(self.ops[e]):
                    waits = self.ops[e][pcs[e]][0]
                    detail.append(
                        f"{e}@{pcs[e]}/{len(self.ops[e])}: waiting "
                        + str([(s.name, v, cnt.get(id(s), 0))
                               for s, v in waits]))
            raise RuntimeError("build-time deadlock:\n" + "\n".join(detail))

    def run_bodies(self, block):
        def body(name):
            def f(eng):
                for waits, fn, incs in self.ops[name]:
                    for s, v in waits:
                        eng.wait_ge(s.h, v)
                    inst = fn(eng)
                    for s, k in incs:
                        inst.then_inc(s.h, k)
            return f
        block.sync(body("sync"))
        block.gpsimd(body("gpsimd"))
        block.vector(body("vector"))
        block.scalar(body("scalar"))
        block.tensor(body("tensor"))


class Builder:
    def __init__(self, nc, st, taps, upto=2):
        self.nc = nc
        self.st = st
        self.taps = taps
        self.upto = upto
        self.P = Prog()
        self.d_taps = {}
        self.din = {}
        for nm, shp in [("xf", [T, D]), ("xq", [1024, D]), ("wq1", [D, D]),
                        ("wq2", [D, D]), ("wk", [D, D]), ("wv", [D, D]),
                        ("wo", [D, D]), ("wlam", [D, H]),
                        ("w1t", [D, INNER]), ("w2t", [INNER, D])]:
            self.din[nm] = nc.dram_tensor(nm, shp, F32,
                                          kind="ExternalInput").ap()
        self.d_id = nc.dram_tensor("ident", [128, 128], BF16,
                                   kind="ExternalInput").ap()
        self.d_cm = nc.dram_tensor("cmask", [NQ * 128, 512], BF16,
                                   kind="ExternalInput").ap()
        self.d_out = nc.dram_tensor("out", [1024, D], F32,
                                    kind="ExternalOutput").ap()
        self.d_sxqt = nc.dram_tensor("sp_xqt", [D, 1024], BF16,
                                     kind="Internal").ap()
        self.d_spart = nc.dram_tensor("sp_part", [1024, D], F32,
                                      kind="Internal").ap()

    def sb(self, n, s, d, stack=None):
        return (stack or self.st).enter_context(self.nc.sbuf_tensor(n, s, d))

    def psb(self, n, s, d, stack=None):
        return (stack or self.st).enter_context(self.nc.psum_tensor(n, s, d))

    def mktap(self, name, shape, dtype):
        if name in self.taps:
            ap = self.nc.dram_tensor("tap_" + name, shape, dtype,
                                     kind="ExternalOutput").ap()
            self.d_taps[name] = ap
            return ap
        return None

    def setup_sems(self):
        mk = lambda n: Sem(self.st.enter_context(self.nc.semaphore(n)), n)
        self.all_dma_sems = []
        self.dSY, self.dGP, self.dSC = mk("dSY"), mk("dGP"), mk("dSC")
        self.all_dma_sems += [self.dSY, self.dGP, self.dSC]
        self.sV, self.sA, self.sT, self.sP = (mk("sV"), mk("sA"), mk("sT"),
                                              mk("sP"))

    def dma_fence_waits(self):
        return tuple((s, s.n) for s in self.all_dma_sems)

    def ev(self, fn, waits=(), inc=True):
        return self.P.emit("vector", fn, waits, self.sV if inc else None)

    def ea(self, fn, waits=(), inc=True):
        return self.P.emit("scalar", fn, waits, self.sA if inc else None)

    def et(self, fn, waits=(), inc=True):
        return self.P.emit("tensor", fn, waits, self.sT if inc else None)

    def ep(self, fn, waits=(), inc=True):
        return self.P.emit("gpsimd", fn, waits, self.sP if inc else None)

    def edma(self, q, out, in_, waits=(), sem=None):
        if sem is None:
            sem = {"sync": self.dSY, "gpsimd": self.dGP,
                   "scalar": self.dSC}[q]
        return self.P.emit(q, (lambda o, i: lambda eng: eng.dma_start(
            out=o, in_=i))(out, in_), waits, inc=sem, inc_k=16)

    def mkdsem(self, n):
        s = Sem(self.st.enter_context(self.nc.semaphore(n)), n)
        self.all_dma_sems.append(s)
        return s

    def vgap(self, waits=()):
        j = self.junk1
        return self.ev(lambda e: e.tensor_scalar_mul(
            j[:, 0:1], j[:, 1:2], 1.0), waits)

    def agap(self, waits=()):
        j = self.junk1
        return self.ea(lambda e: e.copy(j[:, 2:3], j[:, 3:4]), waits)

    def pgap(self, waits=()):
        j = self.junk1
        return self.ep(lambda e: e.tensor_scalar(
            j[:, 4:5], j[:, 5:6], 1.0, None, ALU.mult), waits)

    def slot(self, n=1):
        a = self._sl
        self._sl += n
        assert self._sl <= self.NSM, "small pool overflow"
        return self.smp[:, a:a + n]

    def build(self):
        self.setup_sems()
        nc = self.nc
        self.junk1 = self.sb("junk1", [128, 8], F32)
        self.ident = self.sb("identsb", [128, 128], BF16)
        self.onesc = self.sb("onesc", [128, 1], F32)
        self.onesr = self.sb("onesr", [1, 128], F32)
        self.NSM = 264
        self.smp = self.sb("smp", [128, self.NSM], F32)
        self._sl = 0
        self.epsb = self.slot()
        self.fq_sxo = self.slot(8)
        self.g1v = self.slot(8)
        self.hsv = self.slot(8)
        self.mw1p = self.slot(16)
        self.mw2p = self.slot(16)
        self.sw1 = self.slot()
        self.sw2 = self.slot()
        self.mean1 = self.slot()
        self.mean2 = self.slot()

        self.s01 = _Scope()
        s01 = self.s01.stack(self.nc)
        self.xnT = self.sb("xnT", [128, 16 * T], BF16, s01)
        self.xnTq = self.sb("xnTq", [128, 16 * 1024], BF16, s01)
        self.yT = self.sb("yT", [128, 16 * 1024], BF16, s01)
        self.lam = self.sb("lam", [128, NQ * H], F32, s01)
        self.masks = self.sb("maskssb", [128, NQ * 512], BF16, s01)

        self.tok_id = self.edma("sync", self.ident[:], self.d_id)
        self.tok_msk = self.edma(
            "sync", self.masks[:].rearrange("p (t n) -> p t n", t=NQ),
            self.d_cm.rearrange("(t p) n -> p t n", p=128))
        self.ev(lambda e: e.memset(self.onesc[:], 1.0), inc=False)
        self.ev(lambda e: e.memset(self.onesr[:], 1.0), inc=False)
        self.ev(lambda e: e.memset(self.junk1[:], 1.0), inc=False)
        self.ev(lambda e: e.memset(self.epsb, LNEPS), inc=False)
        self.vgap()
        self.CONST = self.vgap()

        self.phase0()
        if self.upto >= 1:
            self.phase1()
        else:
            self.P1F = self.P0F
        self.emit_taps()
        self.s01.close()
        if self.upto >= 2:
            self.phase2()
        else:
            zb = self.sb("zstub", [128, D], F32)
            tz = self.ev(lambda e: e.memset(zb[:], 0.0), waits=self.P1F)
            for tt in range(8):
                self.edma("sync", self.d_out[tt * 128:(tt + 1) * 128, :],
                          zb[:], waits=(tz,))
            self.P.emit("sync", lambda e: e.nop(),
                        waits=self.dma_fence_waits(), inc=self.sP)
        self.P.check_deadlock()
        blk = self.st.enter_context(nc.Block())
        self.P.run_bodies(blk)

    def xnT_c(self, c, sl=slice(None)):
        return self.xnT[:, c * T:(c + 1) * T][:, sl]

    def xnTq_c(self, c, sl=slice(None)):
        return self.xnTq[:, c * 1024:(c + 1) * 1024][:, sl]

    def yT_c(self, c, sl=slice(None)):
        return self.yT[:, c * 1024:(c + 1) * 1024][:, sl]

    def phase0(self):
        p0 = self.st.enter_context(_Scope())
        sb0 = lambda n, s, d: self.sb(n, s, d, p0.stack(self.nc))
        xbuf = [sb0(f"xb{i}", [128, D], F32) for i in range(2)]
        sqscr = sb0("sqscr", [128, D], BF16)
        xnb = sb0("xnb", [128, D], BF16)
        xnq32 = sb0("xnq32", [128, D], F32)
        q_u = sb0("q_u", [128, 1024], F32)
        q_ui = sb0("q_ui", [128, 1024], I32)
        q_c1 = sb0("q_c1", [128, 1024], F32)
        xqb = sb0("xqb", [128, D], BF16)
        stg = sb0("stg", [128, D], BF16)
        st24 = self.slot(24)
        ptrs = [self.psb("ptr0", [128, 512], BF16, p0.stack(self.nc)),
                self.psb("ptr1", [128, 512], BF16, p0.stack(self.nc))]

        def stt_(q, k):
            return st24[:, q * 12 + k:q * 12 + k + 1]

        tiles = [("f", i) for i in range(16)] + [("q", j) for j in range(NQ)]
        dsx = [self.mkdsem("dsx0"), self.mkdsem("dsx1")]
        dstg = self.mkdsem("dstg")
        tok_consumed = {}
        ptr_war = [None, None]
        prev_xnb_free = None
        prev_q_free = None

        for idx, (kind, i) in enumerate(tiles):
            q = idx % 2
            src = (self.din["xf"] if kind == "f" else self.din["xq"])[
                i * 128:(i + 1) * 128, :]
            dtok = self.edma("sync", xbuf[q][:], src,
                             waits=(tok_consumed.get(idx - 2),),
                             sem=dsx[q])
            xb = xbuf[q]
            tA = self.ea(lambda e, xb=xb, q=q: e.activation(
                sqscr[:], xb[:], AF.Square, accum_out=stt_(q, 1)),
                waits=(dtok, prev_xnb_free))
            tV = self.ev(lambda e, xb=xb, q=q: e.reduce_sum(
                stt_(q, 0), xb[:], AX.X), waits=(dtok, tA))
            self.vgap()
            self.ev(lambda e, q=q: e.tensor_scalar_mul(
                stt_(q, 2), stt_(q, 0), -1.0 / D), inc=False)
            self.vgap()
            self.ev(lambda e, q=q: e.tensor_tensor(
                stt_(q, 11), stt_(q, 2), stt_(q, 2), ALU.mult), inc=False)
            self.ev(lambda e, q=q: e.tensor_scalar(
                stt_(q, 3), stt_(q, 1), 1.0 / D, None, ALU.mult), inc=False)
            self.vgap()
            self.ev(lambda e, q=q: e.tensor_sub(
                stt_(q, 3), stt_(q, 3), stt_(q, 11)), inc=False)
            self.vgap()
            tV2 = self.ev(lambda e, q=q: e.tensor_scalar_max(
                stt_(q, 3), stt_(q, 3), 0.0))
            self.vgap()
            self.vgap()
            self.ea(lambda e, q=q: e.activation(
                stt_(q, 11), stt_(q, 3), AF.Sqrt, bias=self.epsb),
                waits=(tV2,), inc=False)
            tA3 = self.ea(lambda e, q=q: e.activation(
                stt_(q, 6), stt_(q, 3), AF.Sqrt))
            self.agap()
            self.ev(lambda e, q=q: e.reciprocal(stt_(q, 4), stt_(q, 11)),
                    waits=(tA3,), inc=False)
            self.vgap()
            self.ev(lambda e, q=q: e.tensor_tensor(
                stt_(q, 5), stt_(q, 2), stt_(q, 4), ALU.mult), inc=False)
            self.vgap()
            tV3 = self.vgap()
            tapb = self.ea(lambda e, xb=xb, q=q: e.activation(
                xnb[:], xb[:], AF.Identity, bias=stt_(q, 5),
                scale=stt_(q, 4)), waits=(tV3, prev_xnb_free))
            tok_consumed[idx] = tapb
            tapf = None
            if kind == "q":
                tapf = self.ea(lambda e, xb=xb, q=q: e.activation(
                    xnq32[:], xb[:], AF.Identity, bias=stt_(q, 5),
                    scale=stt_(q, 4)), waits=(prev_q_free,))
                tok_consumed[idx] = tapf

            W = T if kind == "f" else 1024
            dstT = self.xnT if kind == "f" else self.xnTq
            last_tt = None
            for g in range(4):
                pp = ptrs[g % 2]
                tts = None
                for c4 in range(4):
                    c = g * 4 + c4
                    w = ([tapb, self.CONST, self.tok_id, ptr_war[g % 2]]
                         if c4 == 0 else [])
                    tts = self.et(lambda e, c=c, c4=c4, pp=pp: e.transpose(
                        pp[:, c4 * 128:(c4 + 1) * 128],
                        xnb[:, c * 128:(c + 1) * 128], self.ident[:]),
                        waits=w, inc=(c4 == 3))
                dst = dstT[:].rearrange("p (c t) -> p c t", c=16)[
                    :, g * 4:(g + 1) * 4, i * 128:(i + 1) * 128]
                tev = self.ev(lambda e, dst=dst, pp=pp: e.tensor_copy(
                    dst, pp[:, 0:512].rearrange("p (c n) -> p c n", c=4)),
                    waits=(tts,))
                ptr_war[g % 2] = tev
                last_tt = tts
            prev_xnb_free = last_tt

            if kind == "q":
                j = i
                tS = self.ev(lambda e, q=q: e.reduce_max(
                    stt_(q, 7), xnq32[:], AX.X, apply_absolute_value=True),
                    waits=(tapf,))
                self.vgap()
                self.ev(lambda e, q=q: e.tensor_tensor(
                    stt_(q, 8), stt_(q, 6), stt_(q, 4), ALU.mult), inc=False)
                self.vgap()
                self.ev(lambda e, q=q: e.tensor_scalar(
                    stt_(q, 8), stt_(q, 8), math.sqrt(D), 1e-12, ALU.mult,
                    ALU.max), inc=False)
                self.vgap()
                self.ev(lambda e, q=q: e.reciprocal(stt_(q, 10), stt_(q, 8)),
                        inc=False)
                self.vgap()
                self.ev(lambda e, q=q: e.tensor_tensor(
                    stt_(q, 9), stt_(q, 7), stt_(q, 10), ALU.mult),
                    waits=(tS,), inc=False)
                self.vgap()
                self.ev(lambda e, q=q: e.tensor_scalar(
                    stt_(q, 9), stt_(q, 9), math.sqrt(D), QEPS, ALU.mult,
                    ALU.max), inc=False)
                self.vgap()
                self.ev(lambda e, j=j, q=q: e.tensor_copy(
                    self.fq_sxo[:, j:j + 1], stt_(q, 9)), inc=False)
                self.vgap()
                self.ev(lambda e, q=q: e.reciprocal(stt_(q, 11), stt_(q, 9)),
                        inc=False)
                self.vgap()
                self.ev(lambda e, q=q: e.tensor_tensor(
                    stt_(q, 11), stt_(q, 11), stt_(q, 10), ALU.mult),
                    inc=False)
                self.vgap()
                tSc = self.ev(lambda e, q=q: e.tensor_scalar_mul(
                    stt_(q, 11), stt_(q, 11), 127.0 * math.sqrt(D)))
                self.vgap()
                self.vgap()
                tQ = None
                for hh in range(2):
                    hsl = slice(hh * 1024, (hh + 1) * 1024)
                    self.ev(lambda e, q=q, hsl=hsl: e.tensor_scalar(
                        q_u[:], xnq32[:, hsl], stt_(q, 11), BIGC + 0.5,
                        ALU.mult, ALU.add),
                        waits=(tSc, prev_q_free) if hh == 0 else (),
                        inc=False)
                    self.ev(lambda e: e.tensor_copy(q_ui[:], q_u[:]),
                            inc=False)
                    self.ev(lambda e: e.tensor_copy(q_c1[:], q_ui[:]),
                            inc=False)
                    self.ev(lambda e: e.tensor_tensor(
                        q_u[:], q_c1[:], q_u[:], ALU.is_gt), inc=False)
                    tQ = self.ev(lambda e, hsl=hsl: e.scalar_tensor_tensor(
                        xqb[:, hsl], q_c1[:], -BIGC, q_u[:], ALU.add,
                        ALU.subtract))
                tstg = None
                for g in range(4):
                    pp = ptrs[g % 2]
                    tts = None
                    for c4 in range(4):
                        c = g * 4 + c4
                        w = ([tQ, prev_xnb_free, ptr_war[g % 2]]
                             if c4 == 0 else [])
                        tts = self.et(
                            lambda e, c=c, c4=c4, pp=pp: e.transpose(
                                pp[:, c4 * 128:(c4 + 1) * 128],
                                xqb[:, c * 128:(c + 1) * 128],
                                self.ident[:]),
                            waits=w, inc=(c4 == 3))
                    tev = self.ev(lambda e, g=g, pp=pp: e.tensor_copy(
                        stg[:, g * 512:(g + 1) * 512], pp[:, 0:512]),
                        waits=(tts,))
                    ptr_war[g % 2] = tev
                    tstg = tev
                prev_q_free = self.edma(
                    "gpsimd",
                    self.d_sxqt.rearrange("(c p) n -> p c n", p=128)[
                        :, :, j * 128:(j + 1) * 128],
                    stg[:].rearrange("p (c n) -> p c n", c=16),
                    waits=(tstg,), sem=dstg)

        self.vgap()
        f_v = self.vgap()
        self.agap()
        f_a = self.agap()
        f_t = self.et(lambda e: e.transpose(
            ptrs[0][:, 0:128], self.ident[:], self.ident[:]),
            waits=(ptr_war[0],))
        f_p = self.pgap()
        f_sy = self.P.emit("sync", lambda e: e.nop(),
                           waits=self.dma_fence_waits(), inc=self.sP)
        self.P0F = (f_v, f_a, f_t, f_p, f_sy)
        p0.close()

    def phase1(self):
        P0F = self.P0F
        nc = self.nc
        p1o = _Scope()
        s1o = p1o.stack(nc)
        p1a = _Scope()
        s1a = p1a.stack(nc)

        mslab = [self.sb(f"msl{i}", [128, 1024], F32, s1o) for i in range(2)]
        mjunk = self.sb("mjunk", [128, 1024], BF16, s1o)
        smp1t = self.sb("smp1", [128, 256], F32, s1o)
        _s1 = [0]

        def slot1(n=1):
            a = _s1[0]
            _s1[0] += n
            assert _s1[0] <= 256
            return smp1t[:, a:a + n]

        nmP = [slot1(16), slot1(16)]
        lP = [slot1(16), slot1(16)]
        e_ = slot1(16)
        d_ = slot1(16)
        lw = slot1(16)
        L_ = slot1(4)
        r_ = slot1(4)
        sf = slot1(16)
        nl = slot1(2)
        nmc = slot1(4)
        nmt = slot1(8)
        msc = slot1(2)

        wrot = [self.sb(f"wrot{i}", [128, 16 * 256], BF16, s1a)
                for i in range(2)]
        q1b = self.sb("q1b", [128, 2 * 1024], BF16, s1a)
        q2b = self.sb("q2b", [128, 2 * 1024], BF16, s1a)
        kbf = [self.sb(f"kbf{i}", [128, T], BF16, s1a) for i in range(2)]
        vbf = [self.sb(f"vbf{i}", [128, T], BF16, s1a) for i in range(2)]
        pbuf = [self.sb(f"pbuf{i}", [128, 512], BF16, s1a) for i in range(3)]
        pTb = [self.sb(f"pTb{i}", [128, 512], BF16, s1a) for i in range(3)]
        ysg = self.sb("ysg", [128, 16 * 128], F32, s1a)
        ystg = self.sb("ystg", [128, 256], BF16, s1a)
        ytmp = [self.sb(f"ytmp{i}", [128, 128], F32, s1a) for i in range(2)]
        wlamb = self.sb("wlamb", [128, 16 * 16], BF16, s1a)

        stash = [self.psb(f"stash{i}", [128, 512], F32, s1a)
                 for i in range(3)]
        trp = [self.psb(f"trp{i}", [128, 512], BF16, s1a) for i in range(2)]
        yp = [self.psb(f"yp{i}", [128, 512], F32, s1a) for i in range(2)]
        prp = self.psb("prp", [128, 512], F32, s1a)

        ev, ea, et, ep, edma = self.ev, self.ea, self.et, self.ep, self.edma
        vgap, agap = self.vgap, self.agap
        xnT_c, xnTq_c = self.xnT_c, self.xnTq_c
        masks, lam, ident = self.masks, self.lam, self.ident

        mean_srcs = []
        for mat, dsrc in (("1", self.din["w1t"]), ("2", self.din["w2t"])):
            for rc in range(dsrc.tensor.shape[0] // 128):
                for cs in range(dsrc.tensor.shape[1] // 1024):
                    mean_srcs.append(
                        (mat, dsrc[rc * 128:(rc + 1) * 128,
                                   cs * 1024:(cs + 1) * 1024]))
        macc1, macc2 = self.mw1p, self.mw2p
        dmsl = [self.mkdsem("dmsl0"), self.mkdsem("dmsl1")]
        dwr = [self.mkdsem("dwr0"), self.mkdsem("dwr1")]
        dwlam = self.mkdsem("dwlam")
        dwob = [self.mkdsem("dwob0"), self.mkdsem("dwob1")]
        dxpt = [self.mkdsem("dxpt0"), self.mkdsem("dxpt1")]
        dspt = self.mkdsem("dspt")
        mtmp = self.slot(8)
        mst = {"i": 0, "tmp_tok": {}, "done": [], "zeroed": False}

        def emit_mean_step():
            i = mst["i"]
            if i >= len(mean_srcs):
                return
            mst["i"] += 1
            if not mst["zeroed"]:
                ev(lambda e: e.memset(macc1[:, :], 0.0), waits=P0F,
                   inc=False)
                ev(lambda e: e.memset(macc2[:, :], 0.0), inc=False)
                mst["zeroed"] = True
            par = i % 2
            mat, srcap = mean_srcs[i]
            dtok = edma("scalar", mslab[par][:], srcap,
                        waits=(mst["done"][i - 2] if i >= 2 else P0F[1],),
                        sem=dmsl[par])
            ts_ = mtmp[:, i % 8:i % 8 + 1]
            atok = ea(lambda e, par=par, ts_=ts_: e.activation(
                mjunk[:], mslab[par][:], AF.Abs, accum_out=ts_),
                waits=(dtok, mst["tmp_tok"].get(i % 8)))
            mst["done"].append(atok)
            acc = macc1 if mat == "1" else macc2
            k = i % 16
            vtok = ev(lambda e, acc=acc, k=k, ts_=ts_: e.tensor_add(
                acc[:, k:k + 1], acc[:, k:k + 1], ts_), waits=(atok,))
            mst["tmp_tok"][i % 8] = vtok

        tok_wlam = edma("gpsimd", wlamb[:].rearrange(
            "p (c n) -> p c n", c=16),
            self.din["wlam"].rearrange("(c p) n -> p c n", p=128),
            waits=(P0F[4],), sem=dwlam)
        prp_war = None
        for tt in range(8):
            mmt = None
            for ic in range(16):
                w = [P0F[2], tok_wlam, prp_war, self.CONST] if ic == 0 else []
                mmt = et(lambda e, ic=ic, tt=tt: e.matmul(
                    prp[:, 0:16],
                    xnTq_c(ic, slice(tt * 128, (tt + 1) * 128)),
                    wlamb[:, ic * 16:(ic + 1) * 16],
                    start=(ic == 0), stop=(ic == 15)), waits=w,
                    inc=(ic == 15))
            prp_war = ea(lambda e, tt=tt: e.activation(
                lam[:, tt * 16:(tt + 1) * 16], prp[:, 0:16], AF.Sigmoid),
                waits=(mmt,))
        lam_gate = agap()

        ev(lambda e: e.memset(nmP[0][:, :], 0.0), waits=P0F, inc=False)
        ev(lambda e: e.memset(nmP[1][:, :], 0.0), inc=False)
        vgap()
        nm0_gate = vgap()

        w_war = [None, None]
        stash_free = [None, None, None]
        pb_free = [None, None, None]
        pT_tok = [None, None, None]
        yp_free = [None, None]
        trp_war = [None, None]
        nm_war = {}
        l_war = {}
        prev_combine = None
        gseg = [0]

        SEGLIST = []
        for t in range(NQ):
            for sig in range(4):
                for s in range(NKB[t]):
                    SEGLIST.append((t, sig, s))

        def emit_QK(t, sig, s, i, gate):
            par = i % 3
            tp = t % 2
            he = sig >> 1
            qb = q1b if (sig & 1) == 0 else q2b
            kv = sig & 1
            idx = s * 4 + sig
            mmt = et(lambda e, par=par, s=s, he=he, t=t, qb=qb,
                     kv=kv: e.matmul(
                stash[par][:, 0:512],
                qb[:, he * 1024 + t * 128:he * 1024 + (t + 1) * 128],
                kbf[kv][:, s * 512:(s + 1) * 512],
                start=True, stop=True),
                waits=(stash_free[par], gate), inc=True)
            vt = mmt
            if s == NKB[t] - 1:
                ev(lambda e, par=par, t=t: e.tensor_tensor(
                    stash[par][:, 0:512], stash[par][:, 0:512],
                    masks[:, t * 512:(t + 1) * 512], ALU.add),
                    waits=(mmt, self.tok_msk), inc=False)
                vt = vgap()
            ext = ea(lambda e, par=par, tp=tp, idx=idx: e.activation(
                pbuf[par][:], stash[par][:, 0:512], AF.Exp,
                bias=nmP[tp][:, idx:idx + 1], scale=1.0,
                accum_out=lP[tp][:, idx:idx + 1]),
                waits=(vt, pb_free[par], l_war.get((tp, sig, s)),
                       nm0_gate if i == 0 else None))
            stash_free[par] = ext
            return ext

        def emit_TRPV(t, sig, s, i, exp_tok):
            par = i % 3
            tpp = i % 2
            kv = sig & 1
            trt = None
            for kc in range(4):
                w = [exp_tok, trp_war[tpp]] if kc == 0 else []
                trt = et(lambda e, kc=kc, par=par, tpp=tpp: e.transpose(
                    trp[tpp][:, kc * 128:(kc + 1) * 128],
                    pbuf[par][:, kc * 128:(kc + 1) * 128], ident[:]),
                    waits=w, inc=(kc == 3))
            tev = ev(lambda e, par=par, tpp=tpp: e.tensor_copy(
                pTb[par][:], trp[tpp][:]), waits=(trt, pT_tok[par]))
            trp_war[tpp] = tev
            pvt = None
            for kc in range(4):
                w = [tev, yp_free[tpp]] if kc == 0 else []
                pvt = et(lambda e, kc=kc, par=par, s=s, kv=kv, tpp=tpp:
                         e.matmul(
                    yp[tpp][:, 0:128],
                    pTb[par][:, kc * 128:(kc + 1) * 128],
                    vbf[kv][:, (4 * s + kc) * 128:(4 * s + kc + 1) * 128],
                    start=(kc == 0), stop=(kc == 3)),
                    waits=w, inc=(kc == 3))
            pb_free[par] = pvt
            pT_tok[par] = pvt
            idx = s * 4 + sig
            yev = ev(lambda e, tpp=tpp, idx=idx: e.tensor_copy(
                ysg[:, idx * 128:(idx + 1) * 128],
                yp[tpp][:, 0:128]), waits=(pvt,))
            yp_free[tpp] = yev
            return yev

        def emit_combine(pj, t, yev_tok):
            nonlocal prev_combine
            ns = NKB[t]
            tp = t % 2
            nm0 = nmP[tp]
            l0 = lP[tp]
            ag = agap()
            if ns == 1:
                ev(lambda e: e.reciprocal(sf[:, 0:4], l0[:, 0:4]),
                   waits=(ag, yev_tok, lam_gate), inc=False)
                vgap()
                vgap()
            else:
                if ns == 2:
                    ev(lambda e: e.tensor_tensor(
                        nmc, nm0[:, 0:4], nm0[:, 4:8], ALU.min),
                        waits=(ag, yev_tok, lam_gate), inc=False)
                elif ns == 3:
                    ev(lambda e: e.tensor_tensor(
                        nmt[:, 0:4], nm0[:, 0:4], nm0[:, 4:8], ALU.min),
                        waits=(ag, yev_tok, lam_gate), inc=False)
                    vgap()
                    ev(lambda e: e.tensor_tensor(
                        nmc, nmt[:, 0:4], nm0[:, 8:12], ALU.min),
                        inc=False)
                else:
                    ev(lambda e: e.tensor_tensor(
                        nmt[:, 0:4], nm0[:, 0:4], nm0[:, 4:8], ALU.min),
                        waits=(ag, yev_tok, lam_gate), inc=False)
                    ev(lambda e: e.tensor_tensor(
                        nmt[:, 4:8], nm0[:, 8:12], nm0[:, 12:16], ALU.min),
                        inc=False)
                    vgap()
                    ev(lambda e: e.tensor_tensor(
                        nmc, nmt[:, 0:4], nmt[:, 4:8], ALU.min), inc=False)
                vgap()
                vgap()
                for s in range(ns):
                    ev(lambda e, s=s: e.tensor_sub(
                        e_[:, s * 4:(s + 1) * 4], nmc,
                        nm0[:, s * 4:(s + 1) * 4]), inc=False)
                vgap()
                t1 = vgap()
                ea(lambda e, ns=ns: e.activation(
                    d_[:, 0:4 * ns], e_[:, 0:4 * ns], AF.Exp),
                    waits=(t1,), inc=False)
                ag2 = agap()
                ev(lambda e, ns=ns: e.tensor_tensor(
                    lw[:, 0:4 * ns], l0[:, 0:4 * ns], d_[:, 0:4 * ns],
                    ALU.mult), waits=(ag2,), inc=False)
                vgap()
                vgap()
                if ns == 2:
                    ev(lambda e: e.tensor_add(L_, lw[:, 0:4], lw[:, 4:8]),
                       inc=False)
                elif ns == 3:
                    ev(lambda e: e.tensor_add(nmt[:, 0:4], lw[:, 0:4],
                                              lw[:, 4:8]), inc=False)
                    vgap()
                    vgap()
                    ev(lambda e: e.tensor_add(L_, nmt[:, 0:4],
                                              lw[:, 8:12]), inc=False)
                else:
                    ev(lambda e: e.tensor_add(nmt[:, 0:4], lw[:, 0:4],
                                              lw[:, 4:8]), inc=False)
                    ev(lambda e: e.tensor_add(nmt[:, 4:8], lw[:, 8:12],
                                              lw[:, 12:16]), inc=False)
                    vgap()
                    vgap()
                    ev(lambda e: e.tensor_add(L_, nmt[:, 0:4],
                                              nmt[:, 4:8]), inc=False)
                vgap()
                vgap()
                ev(lambda e: e.reciprocal(r_, L_), inc=False)
                vgap()
                vgap()
                for s in range(ns):
                    ev(lambda e, s=s: e.tensor_tensor(
                        sf[:, s * 4:(s + 1) * 4], d_[:, s * 4:(s + 1) * 4],
                        r_, ALU.mult), inc=False)
                vgap()
                vgap()
            for he in range(2):
                hcol = t * 16 + 2 * pj + he
                ev(lambda e, he=he, hcol=hcol: e.tensor_scalar_mul(
                    nl[:, he:he + 1], lam[:, hcol:hcol + 1], -1.0),
                    inc=False)
            vgap()
            vgap()
            for s in range(ns):
                for he in range(2):
                    c = s * 4 + 1 + 2 * he
                    ev(lambda e, c=c, he=he: e.tensor_tensor(
                        sf[:, c:c + 1], sf[:, c:c + 1],
                        nl[:, he:he + 1], ALU.mult), inc=False)
            vgap()
            sfg = vgap()
            toks = []
            for he in range(2):
                tmp = ytmp[he]
                terms = [(s * 4 + 2 * he + j, s)
                         for s in range(ns) for j in range(2)]
                first = True
                tk = None
                for n_, (idx, s) in enumerate(terms):
                    ysl = ysg[:, idx * 128:(idx + 1) * 128]
                    sfc = sf[:, idx:idx + 1]
                    last = (n_ == len(terms) - 1)
                    dst = (ystg[:, he * 128:(he + 1) * 128] if last
                           else tmp[:])
                    if first:
                        tk = ev(lambda e, ysl=ysl, sfc=sfc, dst=dst:
                                e.tensor_scalar_mul(dst, ysl, sfc),
                                waits=(sfg,), inc=last)
                        first = False
                    else:
                        tk = ev(lambda e, ysl=ysl, sfc=sfc, dst=dst,
                                tmp=tmp: e.scalar_tensor_tensor(
                            dst, ysl, sfc, tmp[:], ALU.mult, ALU.add),
                            inc=last)
                toks.append(tk)
            trt = None
            for he in range(2):
                trt = et(lambda e, he=he: e.transpose(
                    trp[0][:, he * 128:(he + 1) * 128],
                    ystg[:, he * 128:(he + 1) * 128], ident[:]),
                    waits=(toks[1], trp_war[0]) if he == 0 else (),
                    inc=(he == 1))
            dst = self.yT[:].rearrange("p (c n) -> p c n", c=16)[
                :, 2 * pj:2 * pj + 2, t * 128:(t + 1) * 128]
            tev = ev(lambda e, dst=dst: e.tensor_copy(
                dst, trp[0][:, 0:256].rearrange("p (c n) -> p c n", c=2)),
                waits=(trt,))
            trp_war[0] = tev
            prev_combine = tev
            for s in range(ns):
                for sig in range(4):
                    l_war[(tp, sig, s)] = tev
            return tev

        for pj in range(8):
            c0 = 2 * pj * 128
            tq1 = edma("gpsimd", wrot[0][:].rearrange(
                "p (c n) -> p c n", c=16),
                self.din["wq1"].rearrange("(c p) n -> p c n", p=128)[
                    :, :, c0:c0 + 256], waits=(w_war[0], P0F[4]),
                sem=dwr[0])
            tq2 = edma("gpsimd", wrot[1][:].rearrange(
                "p (c n) -> p c n", c=16),
                self.din["wq2"].rearrange("(c p) n -> p c n", p=128)[
                    :, :, c0:c0 + 256], waits=(w_war[1], P0F[4]),
                sem=dwr[1])
            lastq1 = None
            for he in range(2):
                for th in range(2):
                    mmt = None
                    for ic in range(16):
                        w = [tq1, prp_war, P0F[2]] if ic == 0 else []
                        mmt = et(lambda e, ic=ic, he=he, th=th: e.matmul(
                            prp[:, 0:512],
                            wrot[0][:, ic * 256 + he * 128:
                                     ic * 256 + (he + 1) * 128],
                            xnTq_c(ic, slice(th * 512, (th + 1) * 512)),
                            start=(ic == 0), stop=(ic == 15)),
                            waits=w, inc=(ic == 15))
                    lastq1 = mmt
                    prp_war = ea(lambda e, he=he, th=th: e.activation(
                        q1b[:, he * 1024 + th * 512:he * 1024 +
                            (th + 1) * 512], prp[:, 0:512], AF.Identity,
                        scale=ISQD), waits=(mmt,))
            w_war[0] = lastq1
            tkA = edma("gpsimd", wrot[0][:].rearrange(
                "p (c n) -> p c n", c=16)[:, :, 0:128],
                self.din["wk"].rearrange("(c p) n -> p c n", p=128)[
                    :, :, pj * 128:(pj + 1) * 128],
                waits=(w_war[0],), sem=dwr[0])
            tkB = edma("gpsimd", wrot[0][:].rearrange(
                "p (c n) -> p c n", c=16)[:, :, 128:256],
                self.din["wk"].rearrange("(c p) n -> p c n", p=128)[
                    :, :, (8 + pj) * 128:(9 + pj) * 128], sem=dwr[0])
            lastq2 = None
            for he in range(2):
                for th in range(2):
                    mmt = None
                    for ic in range(16):
                        w = [tq2, prp_war] if ic == 0 else []
                        mmt = et(lambda e, ic=ic, he=he, th=th: e.matmul(
                            prp[:, 0:512],
                            wrot[1][:, ic * 256 + he * 128:
                                     ic * 256 + (he + 1) * 128],
                            xnTq_c(ic, slice(th * 512, (th + 1) * 512)),
                            start=(ic == 0), stop=(ic == 15)),
                            waits=w, inc=(ic == 15))
                    lastq2 = mmt
                    prp_war = ea(lambda e, he=he, th=th: e.activation(
                        q2b[:, he * 1024 + th * 512:he * 1024 +
                            (th + 1) * 512], prp[:, 0:512], AF.Identity,
                        scale=ISQD), waits=(mmt,))
            w_war[1] = lastq2
            tvA = edma("gpsimd", wrot[1][:].rearrange(
                "p (c n) -> p c n", c=16)[:, :, 0:128],
                self.din["wv"].rearrange("(c p) n -> p c n", p=128)[
                    :, :, pj * 128:(pj + 1) * 128],
                waits=(w_war[1],), sem=dwr[1])
            tvB = edma("gpsimd", wrot[1][:].rearrange(
                "p (c n) -> p c n", c=16)[:, :, 128:256],
                self.din["wv"].rearrange("(c p) n -> p c n", p=128)[
                    :, :, (8 + pj) * 128:(9 + pj) * 128], sem=dwr[1])
            lastk = None
            for hx in range(2):
                for tq in range(4):
                    mmt = None
                    for ic in range(16):
                        w = [tkA, tkB, prp_war] if ic == 0 else []
                        mmt = et(lambda e, ic=ic, hx=hx, tq=tq: e.matmul(
                            prp[:, 0:512],
                            wrot[0][:, ic * 256 + hx * 128:
                                     ic * 256 + (hx + 1) * 128],
                            xnT_c(ic, slice(tq * 512, (tq + 1) * 512)),
                            start=(ic == 0), stop=(ic == 15)),
                            waits=w, inc=(ic == 15))
                    lastk = mmt
                    prp_war = ev(lambda e, hx=hx, tq=tq: e.tensor_copy(
                        kbf[hx][:, tq * 512:(tq + 1) * 512], prp[:, 0:512]),
                        waits=(mmt,))
            w_war[0] = lastk
            lastv = None
            for tv in range(16):
                mmt = None
                for ic in range(16):
                    w = [tvA, tvB, prp_war] if ic == 0 else []
                    mmt = et(lambda e, ic=ic, tv=tv: e.matmul(
                        prp[:, 0:256],
                        xnT_c(ic, slice(tv * 128, (tv + 1) * 128)),
                        wrot[1][:, ic * 256:(ic + 1) * 256],
                        start=(ic == 0), stop=(ic == 15)),
                        waits=w, inc=(ic == 15))
                lastv = mmt
                ev(lambda e, tv=tv: e.tensor_copy(
                    vbf[0][:, tv * 128:(tv + 1) * 128], prp[:, 0:128]),
                    waits=(mmt,), inc=False)
                prp_war = ev(lambda e, tv=tv: e.tensor_copy(
                    vbf[1][:, tv * 128:(tv + 1) * 128], prp[:, 128:256]))
            w_war[1] = lastv
            pair_gate = prp_war

            for _ in range(4):
                emit_mean_step()

            from collections import deque
            pending = deque()
            for nseg, (t, sig, s) in enumerate(SEGLIST):
                i = gseg[0]
                ext = emit_QK(t, sig, s, i,
                              gate=pair_gate if nseg == 0 else None)
                is_end = (sig == 3 and s == NKB[t] - 1)
                pending.append((t, sig, s, i, ext, is_end))
                gseg[0] += 1
                if len(pending) >= 3:
                    (pt, psig, ps, pi, pext, pend) = pending.popleft()
                    yev = emit_TRPV(pt, psig, ps, pi, pext)
                    if pend:
                        emit_combine(pj, pt, yev)
                        for _ in range(3):
                            emit_mean_step()
            while pending:
                (pt, psig, ps, pi, pext, pend) = pending.popleft()
                yev = emit_TRPV(pt, psig, ps, pi, pext)
                if pend:
                    emit_combine(pj, pt, yev)

        while mst["i"] < len(mean_srcs):
            emit_mean_step()

        p1a.close()
        p1w = _Scope()
        s1w = p1w.stack(nc)
        wob = [self.sb(f"wob{i}", [128, 16 * 512], BF16, s1w)
               for i in range(2)]
        xpt = [self.sb(f"xpt{i}", [128, 512], F32, s1w) for i in range(2)]
        pot = [self.sb(f"pot{i}", [128, 512], F32, s1w) for i in range(2)]
        wop = self.psb("wop", [128, 512], F32, s1w)
        att_v = self.vgap()
        att_a = self.agap()
        att_t = self.P.emit("tensor", lambda e: e.matmul(
            wop[:, 0:16], self.xnTq[:, 0:128], self.xnTq[:, 0:16],
            start=True, stop=True), waits=(), inc=self.sT)
        ATT = (prev_combine, att_v, att_a, att_t)
        wob_war = [ATT, ATT]
        pot_war = [ATT, ATT]
        xpt_war = [ATT, ATT]
        wop_war = att_t
        last_part = None
        for db in range(4):
            ww = wob_war[db % 2]
            ww = ww if isinstance(ww, tuple) and len(ww) and isinstance(
                ww[0], tuple) else (ww,)
            dwo = edma("gpsimd", wob[db % 2][:].rearrange(
                "p (c n) -> p c n", c=16),
                self.din["wo"].rearrange("(c p) n -> p c n", p=128)[
                    :, :, db * 512:(db + 1) * 512],
                waits=ww, sem=dwob[db % 2])
            for tt in range(8):
                i = db * 8 + tt
                mmt = None
                for ic in range(16):
                    w = [dwo, wop_war, prev_combine] if ic == 0 else []
                    mmt = et(lambda e, ic=ic, tt=tt, db=db: e.matmul(
                        wop[:, 0:512],
                        self.yT[:, ic * 1024 + tt * 128:
                                ic * 1024 + (tt + 1) * 128],
                        wob[db % 2][:, ic * 512:(ic + 1) * 512],
                        start=(ic == 0), stop=(ic == 15)),
                        waits=w, inc=(ic == 15))
                wob_war[db % 2] = mmt
                xw = xpt_war[i % 2]
                xw = xw if isinstance(xw, tuple) and len(xw) and isinstance(
                    xw[0], tuple) else (xw,)
                dxp = edma("sync", xpt[i % 2][:],
                           self.din["xq"][tt * 128:(tt + 1) * 128,
                                          db * 512:(db + 1) * 512],
                           waits=xw, sem=dxpt[i % 2])
                pw = pot_war[i % 2]
                pw = pw if isinstance(pw, tuple) and len(pw) and isinstance(
                    pw[0], tuple) else (pw,)
                ptok = ev(lambda e, i=i: e.tensor_add(
                    pot[i % 2][:], wop[:, 0:512], xpt[i % 2][:]),
                    waits=(mmt, dxp) + pw)
                wop_war = ptok
                xpt_war[i % 2] = ptok
                last_part = edma("sync", self.d_spart[
                    tt * 128:(tt + 1) * 128, db * 512:(db + 1) * 512],
                    pot[i % 2][:], waits=(ptok,), sem=dspt)
                pot_war[i % 2] = last_part

        fin = []
        for acc, sw, mn in ((macc1, self.sw1, self.mean1),
                            (macc2, self.sw2, self.mean2)):
            ev(lambda e, acc=acc: e.reduce_sum(
                msc[:, 0:1], acc[:, :], AX.X),
                waits=(mst["done"][-1],), inc=False)
            vgap()
            g1 = vgap()
            mm1 = et(lambda e: e.matmul(
                wop[0:1, 0:1], self.onesc[:], msc[:, 0:1],
                start=True, stop=True), waits=(g1, wop_war))
            cv = ev(lambda e: e.tensor_scalar(
                msc[0:1, 1:2], wop[0:1, 0:1],
                1.0 / (D * INNER), QEPS, ALU.mult, ALU.max), waits=(mm1,))
            vgap()
            g2 = vgap()
            mm2 = et(lambda e: e.matmul(
                wop[:, 4:5], self.onesr[:], msc[0:1, 1:2],
                start=True, stop=True), waits=(g2,))
            cb = ev(lambda e, mn=mn: e.tensor_copy(mn, wop[:, 4:5]),
                    waits=(mm2,))
            vgap()
            rc_ = ev(lambda e, sw=sw, mn=mn: e.reciprocal(sw, mn),
                     waits=(cb,))
            vgap()
            g3 = vgap()
            wop_war = rc_
            fin.append(g3)

        self.vgap(waits=(last_part,))
        f_v = self.vgap()
        self.agap()
        f_a = self.agap()
        f_t = et(lambda e: e.transpose(trp[0][:, 0:128], ident[:],
                                        ident[:]), waits=tuple(trp_war))
        f_p = self.pgap()
        f_sy = self.P.emit("sync", lambda e: e.nop(),
                           waits=self.dma_fence_waits(), inc=self.sP)
        self.P1F = (f_v, f_a, f_t, f_p, f_sy, fin[0], fin[1])
        p1w.close()
        p1o.close()

    def phase2(self):
        nc = self.nc
        P1F = self.P1F
        ev, ea, et, ep, edma = self.ev, self.ea, self.et, self.ep, self.edma
        vgap, agap, pgap = self.vgap, self.agap, self.pgap
        p2 = _Scope()
        s2 = p2.stack(nc)
        h = self.sb("hbuf", [128, 8 * INNER], F16, s2)

        nsw1 = self.slot()
        nsw2 = self.slot()
        bm05 = self.slot()
        bp05 = self.slot()
        bbig = self.slot()
        ev(lambda e: e.memset(bm05, -0.5), inc=False)
        ev(lambda e: e.memset(bp05, 0.5), inc=False)
        ev(lambda e: e.memset(bbig, BIGC + 0.5), inc=False)
        hq_ssq = self.slot(8)
        hq_ma = self.slot(8)
        hq_n = self.slot(8)
        hq_rn = self.slot(8)
        hq_macl = self.slot(8)
        hq_tsc = self.slot(8)
        ev(lambda e: e.tensor_scalar(
            self.g1v, self.fq_sxo, self.mean1, 1.0 / 127.0, ALU.mult,
            ALU.mult), waits=P1F, inc=False)
        ev(lambda e: e.tensor_scalar_mul(nsw1, self.sw1, -1.0), inc=False)
        ev(lambda e: e.tensor_scalar_mul(nsw2, self.sw2, -1.0), inc=False)
        vgap()
        scgate = vgap()

        p2a = _Scope()
        s2a = p2a.stack(nc)
        xqTs = self.sb("xqTs", [128, 16 * 1024], BF16, s2a)
        NW1 = 4
        w1rot = [self.sb(f"w1r{i}", [128, 512], F32, s2a)
                 for i in range(NW1)]
        wqs = [self.sb(f"wqs{i}", [128, 16 * 512], BF16, s2a)
               for i in range(2)]
        gtmp = self.sb("gtmp", [128, 512], F32, s2a)
        pq = [self.sb(f"pq{i}", [128, 512], BF16, s2a) for i in range(4)]
        ps1 = [self.psb(f"ps1_{i}", [128, 512], F32, s2a) for i in range(8)]

        dxq2 = self.mkdsem("dxq2")
        dw1 = [self.mkdsem(f"dw1{i}") for i in range(4)]
        dw2 = [self.mkdsem(f"dw2{i}") for i in range(4)]
        dxp2 = [self.mkdsem("dxp2a"), self.mkdsem("dxp2b")]
        dout = self.mkdsem("dout")
        txq = edma("sync", xqTs[:].rearrange("p (c n) -> p c n", c=16),
                   self.d_sxqt.rearrange("(c p) n -> p c n", p=128),
                   waits=P1F, sem=dxq2)
        w1r_war = [scgate] * NW1
        pq_war = [scgate, scgate]
        wq_war = [None, None]
        ps1_free = [None] * 8
        gt_war = None
        for ob in range(16):
            wq_tok = None
            for ic in range(16):
                dtok = edma("sync", w1rot[ic % NW1][:],
                            self.din["w1t"][ic * 128:(ic + 1) * 128,
                                            ob * 512:(ob + 1) * 512],
                            waits=(w1r_war[ic % NW1],), sem=dw1[ic % NW1])
                w = [dtok, scgate, pq_war[ic % 2]]
                if ic == 0:
                    w.append(wq_war[ob % 2])
                pp0, pp1 = pq[2 * (ic % 2)], pq[2 * (ic % 2) + 1]
                ev(lambda e, pp0=pp0, ic=ic: e.tensor_scalar(
                    pp0[:], w1rot[ic % NW1][:], self.sw1, 0.5, ALU.mult,
                    ALU.is_gt), waits=w, inc=False)
                cpn = ev(lambda e, pp1=pp1, ic=ic: e.tensor_scalar(
                    pp1[:], w1rot[ic % NW1][:], nsw1, 0.5, ALU.mult,
                    ALU.is_gt))
                w1r_war[ic % NW1] = cpn
                ev(lambda e, pp0=pp0, pp1=pp1, ic=ic, ob=ob:
                   e.tensor_sub(
                    wqs[ob % 2][:, ic * 512:(ic + 1) * 512], pp0[:],
                    pp1[:]), inc=False)
                wq_tok = vgap()
                pq_war[ic % 2] = wq_tok
            mm_tok = [None] * 8
            for tt in range(8):
                mmt = None
                for ic in range(16):
                    w = [wq_tok, ps1_free[tt], txq] if ic == 0 else []
                    mmt = et(lambda e, ic=ic, tt=tt, ob=ob: e.matmul(
                        ps1[tt][:, 0:512],
                        xqTs[:, ic * 1024 + tt * 128:
                             ic * 1024 + (tt + 1) * 128],
                        wqs[ob % 2][:, ic * 512:(ic + 1) * 512],
                        start=(ic == 0), stop=(ic == 15)),
                        waits=w, inc=(ic == 15))
                mm_tok[tt] = mmt
            wq_war[ob % 2] = mm_tok[7]
            for tt in range(8):
                a1 = ea(lambda e, tt=tt: e.activation(
                    gtmp[:], ps1[tt][:, 0:512], AF.Gelu,
                    scale=self.g1v[:, tt:tt + 1]),
                    waits=(mm_tok[tt], gt_war))
                ps1_free[tt] = a1
                a2 = ea(lambda e, tt=tt, ob=ob: e.activation(
                    h[:, tt * INNER + ob * 512:tt * INNER + (ob + 1) * 512],
                    gtmp[:], AF.Gelu))
                gt_war = a2
        ffn1_act = gt_war
        p2a.close()
        if self.upto == 3:
            p2.close()
            self._stub_out(ffn1_act)
            return

        p2b = _Scope()
        s2b = p2b.stack(nc)
        hqs = self.sb("hqs", [128, INNER], F16, s2b)
        idf16 = self.sb("idf16", [128, 128], F16, s2b)
        ev(lambda e: e.tensor_copy(idf16[:], self.ident[:]),
           waits=(ffn1_act,), inc=False)
        vgap()
        idf_tok = vgap()
        qu = self.sb("qu2", [128, 2048], F32, s2b)
        qui = self.sb("qui2", [128, 2048], I32, s2b)
        qc1 = self.sb("qc12", [128, 2048], F32, s2b)
        trp2 = self.psb("trp2", [128, 1024], F16, s2b)

        def hsl(tt, a, b):
            return h[:, tt * INNER + a:tt * INNER + b]

        hq_done = []
        trp2_war = None
        hqs_war = None
        for tt in range(8):
            sq = ea(lambda e, tt=tt: e.activation(
                hqs[:], hsl(tt, 0, INNER), AF.Square,
                accum_out=hq_ssq[:, tt:tt + 1]),
                waits=(ffn1_act, hqs_war))
            mx = ev(lambda e, tt=tt: e.reduce_max(
                hq_ma[:, tt:tt + 1], hsl(tt, 0, INNER), AX.X,
                apply_absolute_value=True), waits=(sq,))
            agap()
            sqa = agap()
            ea(lambda e, tt=tt: e.activation(
                hq_n[:, tt:tt + 1], hq_ssq[:, tt:tt + 1], AF.Sqrt),
                waits=(sqa,), inc=False)
            na = agap()
            ev(lambda e, tt=tt: e.tensor_scalar_max(
                hq_n[:, tt:tt + 1], hq_n[:, tt:tt + 1], 1e-12),
                waits=(na, mx), inc=False)
            vgap()
            ev(lambda e, tt=tt: e.reciprocal(
                hq_rn[:, tt:tt + 1], hq_n[:, tt:tt + 1]), inc=False)
            vgap()
            ev(lambda e, tt=tt: e.tensor_tensor(
                hq_macl[:, tt:tt + 1], hq_ma[:, tt:tt + 1],
                hq_rn[:, tt:tt + 1], ALU.mult), inc=False)
            vgap()
            ev(lambda e, tt=tt: e.tensor_scalar(
                hq_macl[:, tt:tt + 1], hq_macl[:, tt:tt + 1],
                math.sqrt(INNER), QEPS, ALU.mult, ALU.max), inc=False)
            vgap()
            ev(lambda e, tt=tt: e.tensor_scalar(
                self.hsv[:, tt:tt + 1], hq_macl[:, tt:tt + 1],
                self.mean2, 1.0 / 127.0, ALU.mult, ALU.mult), inc=False)
            vgap()
            ev(lambda e, tt=tt: e.reciprocal(
                hq_tsc[:, tt:tt + 1], hq_macl[:, tt:tt + 1]), inc=False)
            vgap()
            ev(lambda e, tt=tt: e.tensor_tensor(
                hq_tsc[:, tt:tt + 1], hq_tsc[:, tt:tt + 1],
                hq_rn[:, tt:tt + 1], ALU.mult), inc=False)
            vgap()
            ev(lambda e, tt=tt: e.tensor_scalar_mul(
                hq_tsc[:, tt:tt + 1], hq_tsc[:, tt:tt + 1],
                127.0 * math.sqrt(INNER)), inc=False)
            vgap()
            tscg = vgap()
            lastq = None
            for qq in range(4):
                a0, b0 = qq * 2048, (qq + 1) * 2048
                pu = ea(lambda e, tt=tt, a0=a0, b0=b0: e.activation(
                    qu[:], hsl(tt, a0, b0), AF.Identity,
                    bias=bbig, scale=hq_tsc[:, tt:tt + 1]),
                    waits=(tscg, lastq))
                ev(lambda e: e.tensor_copy(qui[:], qu[:]), waits=(pu,),
                   inc=False)
                ev(lambda e: e.tensor_copy(qc1[:], qui[:]), inc=False)
                ev(lambda e: e.tensor_tensor(qu[:], qc1[:], qu[:],
                                             ALU.is_gt), inc=False)
                lastq = ev(lambda e, a0=a0, b0=b0: e.scalar_tensor_tensor(
                    hqs[:, a0:b0], qc1[:], -BIGC, qu[:], ALU.add,
                    ALU.subtract))
            hw = None
            for g in range(8):
                trt = None
                for k in range(8):
                    oc = g * 8 + k
                    w = [lastq, trp2_war] if k == 0 else []
                    trt = et(lambda e, oc=oc, k=k: e.transpose(
                        trp2[:, k * 128:(k + 1) * 128],
                        hqs[:, oc * 128:(oc + 1) * 128], idf16[:]),
                        waits=w + [idf_tok] if k == 0 else [], inc=(k == 7))
                hw = ev(lambda e, tt=tt, g=g: e.tensor_copy(
                    hsl(tt, g * 1024, (g + 1) * 1024), trp2[:, 0:1024]),
                    waits=(trt,))
                trp2_war = hw
            hqs_war = hw
            hq_done.append(hw)
        p2b.close()
        if self.upto == 4:
            p2.close()
            self._stub_out(hq_done[-1])
            return

        p2c = _Scope()
        s2c = p2c.stack(nc)
        NW2 = 4
        w2rot = [self.sb(f"w2r{i}", [128, 512], F32, s2c)
                 for i in range(NW2)]
        w2q = [self.sb(f"w2q{i}", [128, 512], F16, s2c)
               for i in range(NW2)]
        pq2 = [self.sb(f"pq2_{i}", [128, 512], BF16, s2c) for i in range(4)]
        xpt2 = [self.sb(f"xpt2_{i}", [128, 512], F32, s2c) for i in range(2)]
        pot2 = [self.sb(f"pot2_{i}", [128, 512], F32, s2c) for i in range(2)]
        ps2 = [self.psb(f"ps2_{i}", [128, 512], F32, s2c) for i in range(8)]

        NDB = 1 if self.upto == 5 else (2 if self.upto == 6 else 4)
        hq_gate = hq_done[-1]
        w2r_war = [hq_gate] * NW2
        pq2_war = [hq_gate, hq_gate]
        w2q_war = [hq_gate] * NW2
        ps2_free = [hq_gate] * 8
        xpt2_war = [hq_gate, hq_gate]
        pot2_war = [hq_gate, hq_gate]
        for db in range(NDB):
            mm_last = [None] * 8
            for oc in range(64):
                dtok = edma("sync", w2rot[oc % NW2][:],
                            self.din["w2t"][oc * 128:(oc + 1) * 128,
                                            db * 512:(db + 1) * 512],
                            waits=(w2r_war[oc % NW2],), sem=dw2[oc % NW2])
                w = [dtok, scgate, w2q_war[oc % NW2], pq2_war[oc % 2]]
                qp0, qp1 = pq2[2 * (oc % 2)], pq2[2 * (oc % 2) + 1]
                ev(lambda e, qp0=qp0, oc=oc: e.tensor_scalar(
                    qp0[:], w2rot[oc % NW2][:], self.sw2, 0.5, ALU.mult,
                    ALU.is_gt), waits=w, inc=False)
                cpn = ev(lambda e, qp1=qp1, oc=oc: e.tensor_scalar(
                    qp1[:], w2rot[oc % NW2][:], nsw2, 0.5, ALU.mult,
                    ALU.is_gt))
                w2r_war[oc % NW2] = cpn
                ev(lambda e, qp0=qp0, qp1=qp1, oc=oc:
                   e.tensor_sub(
                    w2q[oc % NW2][:], qp0[:], qp1[:]), inc=False)
                wq2_tok = vgap()
                pq2_war[oc % 2] = wq2_tok
                mmt = None
                for tt in range(8):
                    w = [wq2_tok, hq_gate]
                    if oc == 0:
                        w.append(ps2_free[tt])
                    mmt = et(lambda e, oc=oc, tt=tt, db=db: e.matmul(
                        ps2[tt][:, 0:512],
                        h[:, tt * INNER + oc * 128:
                          tt * INNER + (oc + 1) * 128],
                        w2q[oc % NW2][:], start=(oc == 0), stop=(oc == 63)),
                        waits=w if tt == 0 else ([ps2_free[tt]]
                                                 if oc == 0 else []),
                        inc=(oc == 63 or tt == 7))
                    mm_last[tt] = mmt
                w2q_war[oc % NW2] = mmt
            for tt in range(8):
                i = db * 8 + tt
                dxp = edma("sync", xpt2[i % 2][:],
                           self.d_spart[tt * 128:(tt + 1) * 128,
                                        db * 512:(db + 1) * 512],
                           waits=(xpt2_war[i % 2],), sem=dxp2[i % 2])
                ptk = ev(lambda e, i=i, tt=tt: e.scalar_tensor_tensor(
                    pot2[i % 2][:], ps2[tt][:, 0:512],
                    self.hsv[:, tt:tt + 1], xpt2[i % 2][:],
                    ALU.mult, ALU.add),
                    waits=(mm_last[tt], dxp, pot2_war[i % 2]))
                ps2_free[tt] = ptk
                xpt2_war[i % 2] = ptk
                otok = edma("sync", self.d_out[
                    tt * 128:(tt + 1) * 128, db * 512:(db + 1) * 512],
                    pot2[i % 2][:], waits=(ptk,), sem=dout)
                pot2_war[i % 2] = otok
        if self.upto in (5, 6):
            zb = self.sb("zst3", [128, D - 512 * NDB], F32, s2c)
            tz = self.ev(lambda e: e.memset(zb[:], 0.0),
                         waits=(pot2_war[0], pot2_war[1]))
            for tt in range(8):
                self.edma("sync",
                          self.d_out[tt * 128:(tt + 1) * 128, 512 * NDB:],
                          zb[:], waits=(tz,))
        self.P.emit("sync", lambda e: e.nop(),
                    waits=self.dma_fence_waits(), inc=self.sP)
        p2c.close()
        p2.close()

    def _stub_out(self, gate):
        zb = self.sb("zstub2", [128, D], F32)
        fv = self.vgap(waits=(gate,))
        fa = self.agap()
        fp = self.pgap()
        tz = self.ev(lambda e: e.memset(zb[:], 0.0),
                     waits=(fv, fa, fp))
        for tt in range(8):
            self.edma("sync", self.d_out[tt * 128:(tt + 1) * 128, :],
                      zb[:], waits=(tz,))
        self.P.emit("sync", lambda e: e.nop(),
                    waits=self.dma_fence_waits(), inc=self.sP)

    def emit_taps(self):
        tapdefs = {
            "xnT": (self.xnT[:], [128, 16 * T], BF16),
            "xnTq": (self.xnTq[:], [128, 16 * 1024], BF16),
            "fq_sxo": (self.fq_sxo, [128, 8], F32),
            "yT": (self.yT[:], [128, 16 * 1024], BF16),
            "lam": (self.lam[:], [128, NQ * H], F32),
            "sxo2": (self.smp[:, 0:64], [128, 64], F32),
        }
        for name in self.taps:
            ap, shape, dt_ = tapdefs[name]
            dram = self.mktap(name, shape, dt_)
            self.edma("sync", dram, ap, waits=self.P1F)
        self.P.emit("sync", lambda e: e.nop(),
                    waits=((self.dSY, self.dSY.n),), inc=self.sP)


class _Scope:

    def __init__(self):
        self._st = None

    def stack(self, nc):
        if self._st is None:
            from contextlib import ExitStack
            self._st = ExitStack()
        return self._st

    def close(self):
        if self._st is not None:
            self._st.close()
            self._st = None

    def __enter__(self):
        return self

    def __exit__(self, *a):
        self.close()
        return False


def build_nc(taps=(), upto=2):
    from contextlib import ExitStack
    nc = bass.Bass("TRN2", target_bir_lowering=False,
                   detect_race_conditions=False)
    with ExitStack() as st:
        b = Builder(nc, st, taps, upto)
        b.build()
    return nc, sorted(b.d_taps)


def make_masks(r):
    m = np.zeros((NQ, 128, 512), dtype=np.float32)
    jj = np.arange(512)[None, :]
    pp = np.arange(128)[:, None]
    for t in range(NQ):
        g = 2 * t + r
        delta = g * 128 - (NKB[t] - 1) * 512
        m[t] = np.where(jj <= delta + pp, 0.0, -1e9)
    return m.reshape(NQ * 128, 512).astype(bf16)


def shard_inputs(inputs):
    x = np.asarray(inputs["L0_attn_x"])
    wq1 = np.ascontiguousarray(np.asarray(inputs["L0_attn_Wq1"]))
    wq2 = np.ascontiguousarray(np.asarray(inputs["L0_attn_Wq2"]))
    wk = np.ascontiguousarray(np.asarray(inputs["L0_attn_Wk"]))
    wv = np.ascontiguousarray(np.asarray(inputs["L0_attn_Wv"]))
    wlam = np.ascontiguousarray(np.asarray(inputs["L0_attn_Wlam"]))
    wo = np.ascontiguousarray(np.asarray(inputs["L0_attn_Wo"]))
    w1t = np.ascontiguousarray(np.asarray(inputs["L0_ffn_w1"]).T)
    w2t = np.ascontiguousarray(np.asarray(inputs["L0_ffn_w2"]).T)
    ident = np.eye(128, dtype=bf16)
    in_maps = []
    for c in range(8):
        b, r = c >> 1, c & 1
        rows = np.concatenate([np.arange((2 * t + r) * 128,
                                         (2 * t + r + 1) * 128)
                               for t in range(NQ)])
        in_maps.append({
            "xf": np.ascontiguousarray(x[b]),
            "xq": np.ascontiguousarray(x[b][rows]),
            "wq1": wq1, "wq2": wq2, "wk": wk, "wv": wv, "wo": wo,
            "wlam": wlam, "w1t": w1t, "w2t": w2t,
            "ident": ident, "cmask": make_masks(r),
        })
    return in_maps


def kernel(**inputs):
    nc, _ = build_nc()
    in_maps = shard_inputs(inputs)
    res = run_bass_kernel_spmd(nc, in_maps, core_ids=list(range(8)))
    x = np.asarray(inputs["L0_attn_x"])
    out = np.empty((B, T, D), dtype=np.float32)
    for c in range(8):
        b, r = c >> 1, c & 1
        o = np.asarray(res.results[c]["out"])
        for t in range(NQ):
            out[b, (2 * t + r) * 128:(2 * t + r + 1) * 128, :] = \
                o[t * 128:(t + 1) * 128, :]
    return out
